# version 13
# baseline (speedup 1.0000x reference)
import numpy as np
import ml_dtypes
import concourse.bass as bass
import concourse.mybir as mybir
from concourse.bass_utils import run_bass_kernel_spmd
from concourse.alu_op_type import AluOpType
from contextlib import ExitStack

F32 = mybir.dt.float32
BF16 = mybir.dt.bfloat16
I32 = mybir.dt.int32
AF = mybir.ActivationFunctionType
OP = AluOpType

T = 2048
D = 1024
DI = 2048
DCONV = 2304
DPROJ = 4384
CS = 128
NCH = T // CS
NT = 1024
EPS = 1e-5

ENGS = ("sync", "vector", "scalar", "tensor", "gpsimd")
DMA_SEM = {"sync": "dSY", "vector": "dV", "scalar": "dSC", "tensor": "dT",
           "gpsimd": "dG"}


class Prog:

    def __init__(self, nc, st):
        self.nc = nc
        self.st = st
        self.items = {e: [] for e in ENGS}
        self.semn = {}
        self.semh = {}
        for e in ENGS:
            self.semh[e] = st.enter_context(nc.semaphore("s_" + e))
            self.semn[e] = 0
        self.res = {}
        self.waited = {e: {} for e in ENGS}
        self.nops = 0
        self.nsem = len(ENGS)
        self.dead = False
        self.psum_res = set()

    SEM_GROUP = {}

    def dma_sem(self, eng, res):
        res = self.SEM_GROUP.get(res, res)
        nm = "d|%s|%s" % (eng, res)
        if nm not in self.semh:
            self.semh[nm] = self.st.enter_context(
                self.nc.semaphore("d_%s_%s" % (eng, res)))
            self.semn[nm] = 0
            self.nsem += 1
        return nm

    def _r(self, name):
        return self.res.setdefault(name, {"w": None, "r": []})

    def _need(self, reads, writes):
        needed = {}

        def add(sn, tk):
            if sn.startswith("d|"):
                tk = self.semn[sn]
            needed[sn] = max(needed.get(sn, 0), tk)

        for r in reads:
            w = self._r(r)["w"]
            if w is not None:
                add(*w)
        for r in writes:
            rec = self._r(r)
            if rec["w"] is not None:
                add(*rec["w"])
            for (sn, tk) in rec["r"]:
                add(sn, tk)
        return needed

    def _waits(self, eng, needed):
        wt = self.waited[eng]
        for sn, tk in sorted(needed.items()):
            if wt.get(sn, 0) >= tk:
                continue
            wt[sn] = tk
            h = self.semh[sn]
            self.items[eng].append(lambda e, h=h, v=tk: e.wait_ge(h, v))

    def _update(self, reads, writes, ident):
        for r in reads:
            self._r(r)["r"].append(ident)
        for r in writes:
            rec = self._r(r)
            rec["w"] = ident
            rec["r"] = []

    def op(self, eng, fns, reads=(), writes=()):
        if self.dead:
            return ("sync", 0)
        if callable(fns):
            fns = [fns]
        pr = tuple(r for r in reads if r in self.psum_res)
        if pr:
            reads = tuple(r for r in reads if r not in self.psum_res)
            writes = tuple(writes) + pr
        self._waits(eng, self._need(reads, writes))
        self.semn[eng] += 1
        tok = self.semn[eng]
        h = self.semh[eng]
        for f in fns[:-1]:
            self.items[eng].append(f)
        self.items[eng].append(lambda e, f=fns[-1], h=h: f(e).then_inc(h, 1))
        self.nops += len(fns)
        self._update(reads, writes, (eng, tok))
        return (eng, tok)

    def dma(self, eng, dst, src, reads=(), writes=(), slow=False):
        if self.dead:
            return ("sync", 0)
        assert len(writes) == 1
        self._waits(eng, self._need(reads, writes))
        dname = self.dma_sem(eng, writes[0])
        self.semn[dname] += 16
        tok = self.semn[dname]
        h = self.semh[dname]
        self.items[eng].append(
            lambda e, o=dst, i=src, h=h, sl=slow: e.dma_start(
                out=o, in_=i, allow_slow_non_contiguous=sl).then_inc(h, 16))
        self.nops += 1
        self._update(reads, writes, (dname, tok))
        return (dname, tok)

    def final_wait(self, eng, reads):
        self._waits(eng, self._need(reads, ()))

    def barrier(self):
        if self.dead:
            return
        if not hasattr(self, "bsem"):
            self.bsem = self.st.enter_context(self.nc.semaphore("barrier"))
            self.bn = 0
        for sn, cnt in sorted(self.semn.items()):
            if sn == "sync" or cnt == 0:
                continue
            if self.waited["sync"].get(sn, 0) >= cnt:
                continue
            self.waited["sync"][sn] = cnt
            h = self.semh[sn]
            self.items["sync"].append(lambda e, h=h, v=cnt: e.wait_ge(h, v))
        self.bn += 1
        bh = self.bsem
        bv = self.bn
        self.items["sync"].append(lambda e, h=bh, v=bv: e.nop().then_inc(h, 1))
        self.semn["sync"] += 1
        snap = dict(self.semn)
        for eng in ENGS:
            if eng != "sync":
                self.items[eng].append(lambda e, h=bh, v=bv: e.wait_ge(h, v))
            self.waited[eng] = dict(snap)

    def realize(self):
        with self.nc.Block() as block:
            for eng in ENGS:
                items = self.items[eng]
                if items:
                    getattr(block, eng)(lambda e, it=items: [f(e) for f in it])



_G = {}
for _i in range(16):
    _G["xtm%d" % _i] = "xtmg"
    _G["wo_%d" % _i] = "wog"
for _sl in range(2):
    for _dc in range(8):
        _G["wb%d_%d" % (_sl, _dc)] = "wbg%d" % _sl
for _w in range(6):
    _G["wdq%d" % _w] = "wdqg"
for _nm in ("idb", "idf", "onesb", "onesf", "tri", "lnw", "lnb", "bcol",
            "la_f", "lb_f", "cdf", "cb", "nw", "nb", "dtbr", "alogr"):
    _G[_nm] = "constg"
for _nm in ("hidacum", "lodacum", "hidcombo", "lodcombo", "acumd", "combod"):
    _G[_nm] = "hilog"
for _nm in ("qst", "sst"):
    _G[_nm] = "dqg"
Prog.SEM_GROUP = _G


def build_program(tap=None):
    nc = bass.Bass("TRN2", target_bir_lowering=False)
    st = ExitStack()

    def din(name, shape, dt=F32):
        return nc.dram_tensor(name, shape, dt, kind="ExternalInput").ap()

    xin = din("xin", [D, T])
    win = din("win", [D, DPROJ])
    wqt = din("wqt", [6, D, D], I32)
    srt = din("srt", [6, D, D])
    biasc = din("biasc", [128, 48])
    lat = din("lat", [128, 6 * 8 * 32])
    lbt = din("lbt", [32, 6 * D])
    lnw3 = din("lnw3", [128, 24])
    lnb3 = din("lnb3", [128, 24])
    wout = din("wout", [DI, D])
    nwc = din("nwc", [128, 16])
    nbc = din("nbc", [128, 16])
    cdiag = din("cdiag", [128, 4 * 18 * 128])
    cbias = din("cbias", [128, 18])
    dtb = din("dtb", [32, 1])
    alog = din("alog", [32, 1])
    identb = din("identb", [128, 128], BF16)
    identf = din("identf", [128, 128])
    trimask = din("trimask", [128, 128])
    onesbf = din("onesbf", [128, 128], BF16)
    onesf = din("onesf", [128, 512])

    out_t = nc.dram_tensor("out", [D, T], F32, kind="ExternalOutput").ap()
    wdq = nc.dram_tensor("wdq", [6, D, D], BF16, kind="Internal").ap()
    winb_d = nc.dram_tensor("winb_d", [D, DPROJ], BF16, kind="Internal").ap()
    woutb_d = nc.dram_tensor("woutb_d", [DI, D], BF16, kind="Internal").ap()
    zsil_d = nc.dram_tensor("zsil_d", [T, DI], BF16, kind="Internal").ap()
    xbcpre_d = nc.dram_tensor("xbcpre_d", [DCONV, T], BF16, kind="Internal").ap()
    xtm_d = nc.dram_tensor("xtm_d", [T, DI], BF16, kind="Internal").ap()
    acum_d = nc.dram_tensor("acum_d", [32, T], F32, kind="Internal").ap()
    combo_d = nc.dram_tensor("combo_d", [32, T], F32, kind="Internal").ap()
    acumhi_d = nc.dram_tensor("acumhi_d", [32, T], F32, kind="Internal").ap()
    acumlo_d = nc.dram_tensor("acumlo_d", [32, T], F32, kind="Internal").ap()
    gend_d = nc.dram_tensor("gend_d", [NCH, 32], F32, kind="Internal").ap()

    p = Prog(nc, st)
    TAP = {"done": False}

    def sb(stk, name, shape, dt=F32):
        return stk.enter_context(nc.sbuf_tensor(name, shape, dt))

    def psum(stk, name, shape, dt=F32):
        return stk.enter_context(nc.psum_tensor(name, shape, dt))

    xb = sb(st, "xb", [128, 8 * T], BF16)
    hb = sb(st, "hb", [128, 8 * T], BF16)
    la_sb = sb(st, "la_sb", [128, 6 * 8 * 32], BF16)
    lb_sb = sb(st, "lb_sb", [32, 6 * D], BF16)
    mid_sb = sb(st, "mid_sb", [32, NT], BF16)
    bc_sb = sb(st, "bc_sb", [128, 48])
    lnw_sb = sb(st, "lnw_sb", [128, 24])
    lnb_sb = sb(st, "lnb_sb", [128, 24])
    idb_sb = sb(st, "idb_sb", [128, 128], BF16)
    idf_sb = sb(st, "idf_sb", [128, 128])
    onesb_sb = sb(st, "onesb_sb", [128, 128], BF16)
    onesf_sb = sb(st, "onesf_sb", [128, 512])
    tri_sb = sb(st, "tri_sb", [128, 128])
    xstage = sb(st, "xstage", [128, 2 * NT])

    def mmf(lhsT, rhs, out, start, stop):
        return lambda e, a=lhsT, b=rhs, o=out, s1=start, s2=stop: \
            e.matmul(out=o, lhsT=a, rhs=b, start=s1, stop=s2)

    def do_tap(src_ap, reads):
        TAP["done"] = True
        tp = nc.dram_tensor("tapo", list(src_ap.shape), src_ap.dtype,
                            kind="ExternalOutput").ap()
        p.dma("gpsimd", tp, src_ap, reads=reads, writes=("tapped",))
        p.dead = True

    with ExitStack() as st0:
        la_f = sb(st0, "la_f", [128, 6 * 8 * 32])
        lb_f = sb(st0, "lb_f", [32, 6 * D])
        for dst, src, rn in [
                (idb_sb[:], identb, "idb"), (idf_sb[:], identf, "idf"),
                (onesb_sb[:], onesbf, "onesb"), (onesf_sb[:], onesf, "onesf"),
                (tri_sb[:], trimask, "tri"), (lnw_sb[:], lnw3, "lnw"),
                (lnb_sb[:], lnb3, "lnb"), (bc_sb[:], biasc, "bcol"),
                (la_f[:], lat, "la_f"), (lb_f[:], lbt, "lb_f")]:
            p.dma("sync", dst, src, writes=(rn,))
        p.op("vector", lambda e: e.tensor_copy(out=la_sb[:], in_=la_f[:]),
             reads=("la_f",), writes=("la",))
        p.op("vector", lambda e: e.tensor_copy(out=lb_sb[:], in_=lb_f[:]),
             reads=("lb_f",), writes=("lb",))
        p.dma("gpsimd", out_t, xin, writes=("xd",))
        for k in range(16):
            sl = xstage[:, (k % 2) * NT:(k % 2 + 1) * NT]
            dc, Hh = k // 2, k % 2
            src = xin[dc * 128:(dc + 1) * 128, Hh * NT:(Hh + 1) * NT]
            p.dma("sync", sl, src, writes=("xs%d" % (k % 2),))
            p.op("vector", lambda e, s=sl, d=xb[:, dc * T + Hh * NT: dc * T + Hh * NT + NT]:
                 e.tensor_copy(out=d, in_=s), reads=("xs%d" % (k % 2),), writes=("xb",))

    def emit_ffn(call):
        with ExitStack() as fst:
            gbuf = sb(fst, "gbuf%d" % call, [128, 8 * NT], BF16)
            yraw = sb(fst, "yraw%d" % call, [128, 8 * NT], BF16)
            wbuf = sb(fst, "wbuf%d" % call, [128, 2 * 8 * D], BF16)
            qi_sb = sb(fst, "qi%d" % call, [128, D], I32)
            sr_sb = sb(fst, "sr%d" % call, [128, D])
            qf_sb = sb(fst, "qf%d" % call, [128, D])
            ps_y = [psum(fst, "fy0_%d" % call, [128, NT]),
                    psum(fst, "fy1_%d" % call, [128, NT])]
            ps_s0 = psum(fst, "fs0_%d" % call, [128, NT])
            ps_s1 = psum(fst, "fs1_%d" % call, [128, NT])
            p.psum_res |= {"fy0", "fy1", "fs0", "fs1"}
            y2 = sb(fst, "y2_%d" % call, [128, NT], BF16)
            y2b = sb(fst, "y2b_%d" % call, [128, NT], BF16)
            y2s = [y2, y2b]
            r_rep = sb(fst, "rrep%d" % call, [128, NT], BF16)
            nmr_rep = sb(fst, "nmrep%d" % call, [128, NT], BF16)
            mu_rep = sb(fst, "murep%d" % call, [128, NT])
            m2_rep = sb(fst, "m2rep%d" % call, [128, NT])
            var_rep = sb(fst, "varep%d" % call, [128, NT])

            def stats_rows(ps_sum, rs_sum, ps_sq, rs_sq, nfeat):
                inv = 1.0 / nfeat
                p.op("vector", lambda e: e.tensor_scalar(
                    out=mu_rep[:], in0=ps_sum, scalar1=-inv, scalar2=None,
                    op0=OP.mult), reads=rs_sum, writes=("mu",))
                p.op("vector", lambda e: e.tensor_scalar(
                    out=var_rep[:], in0=ps_sq, scalar1=inv, scalar2=float(EPS),
                    op0=OP.mult, op1=OP.add), reads=rs_sq, writes=("var",))
                if tap == "sA" and not TAP["done"]:
                    do_tap(var_rep[:], ("var",))
                    return
                p.op("scalar", lambda e: e.activation(
                    out=m2_rep[:], in_=ps_sum, func=AF.Square, scale=inv),
                    reads=rs_sum, writes=("m2",))
                p.op("vector", lambda e: e.tensor_tensor(
                    out=var_rep[:], in0=var_rep[:], in1=m2_rep[:], op=OP.subtract),
                    reads=("var", "m2"), writes=("var",))
                if tap == "sB" and not TAP["done"]:
                    do_tap(var_rep[:], ("var",))
                    return
                p.op("scalar", lambda e: e.activation(
                    out=m2_rep[:], in_=var_rep[:], func=AF.Ln),
                    reads=("var",), writes=("m2",))
                p.op("scalar", lambda e: e.activation(
                    out=r_rep[:], in_=m2_rep[:], func=AF.Exp, scale=-0.5),
                    reads=("m2",), writes=("rr",))
                p.op("vector", lambda e: e.tensor_tensor(
                    out=nmr_rep[:], in0=mu_rep[:], in1=r_rep[:], op=OP.mult),
                    reads=("mu", "rr"), writes=("nmr",))
                if tap == "sC" and not TAP["done"]:
                    do_tap(nmr_rep[:], ("nmr",))
                    return

            def top_ln_to(dres):
                for H in range(2):
                    t0 = H * NT
                    for dc in range(8):
                        xs = xb[:, dc * T + t0: dc * T + t0 + NT]
                        p.op("tensor",
                             [mmf(onesb_sb[:], xs[:, 0:512], ps_s0[:, 0:512], dc == 0, dc == 7),
                              mmf(onesb_sb[:], xs[:, 512:NT], ps_s0[:, 512:NT], dc == 0, dc == 7)],
                             reads=("onesb", "xb"), writes=("fs0",))

                    def sq_mms(dc):
                        y2c = y2s[dc % 2]
                        p.op("tensor",
                             [mmf(onesb_sb[:], y2c[:, 0:512], ps_s1[:, 0:512], dc == 0, dc == 7),
                              mmf(onesb_sb[:], y2c[:, 512:NT], ps_s1[:, 512:NT], dc == 0, dc == 7)],
                             reads=("onesb", "y2%d" % (dc % 2)), writes=("fs1",))

                    for dc in range(8):
                        xs = xb[:, dc * T + t0: dc * T + t0 + NT]
                        p.op("scalar", lambda e, s=xs, d=y2s[dc % 2]: e.activation(
                            out=d[:], in_=s, func=AF.Square), reads=("xb",),
                            writes=("y2%d" % (dc % 2),))
                        if dc >= 1:
                            sq_mms(dc - 1)
                    sq_mms(7)
                    stats_rows(ps_s0[:], ("fs0",), ps_s1[:], ("fs1",), D)
                    for dc in range(8):
                        xs = xb[:, dc * T + t0: dc * T + t0 + NT]
                        hs = hb[:, dc * T + t0: dc * T + t0 + NT]
                        y2c = y2s[dc % 2]
                        yr2 = "y2%d" % (dc % 2)
                        p.op("vector", lambda e, a=xs, d=y2c: e.tensor_tensor(
                            out=d[:], in0=a, in1=r_rep[:], op=OP.mult),
                            reads=("xb", "rr"), writes=(yr2,))
                        p.op("vector", lambda e, d=hs, sc=y2c: e.tensor_tensor(
                            out=d, in0=sc[:], in1=nmr_rep[:], op=OP.add),
                            reads=(yr2, "nmr"), writes=(dres,))

            def dequant_weight(w):
                slot = w % 2
                for dc in range(8):
                    p.dma("sync", qi_sb[:], wqt[w, dc * 128:(dc + 1) * 128, :],
                          writes=("qst",))
                    p.dma("sync", sr_sb[:], srt[w, dc * 128:(dc + 1) * 128, :],
                          writes=("sst",))
                    p.op("scalar", lambda e: e.activation(
                        out=qf_sb[:], in_=qi_sb[:], func=AF.Copy, bias=-1.0,
                        scale=1.0 / 7.5), reads=("qst",), writes=("qf",))
                    dst = wbuf[:, slot * 8 * D + dc * D: slot * 8 * D + (dc + 1) * D]
                    p.op("vector", lambda e, d=dst: e.tensor_tensor(
                        out=d, in0=qf_sb[:], in1=sr_sb[:], op=OP.mult),
                        reads=("qf", "sst"), writes=("wb%d_%d" % (slot, dc),))
                    p.dma("gpsimd", wdq[w, dc * 128:(dc + 1) * 128, :], dst,
                          reads=("wb%d_%d" % (slot, dc),), writes=("wdq%d" % w,))

            def stream_weight(w):
                slot = w % 2
                p.dma("scalar",
                      wbuf[:, slot * 8 * D: (slot + 1) * 8 * D]
                      .rearrange("p (g n) -> p g n", n=D),
                      wdq[w].rearrange("(g p) n -> p g n", p=128),
                      reads=("wdq%d" % w,), writes=("wb%d_0" % slot,))

            def lora_mid(widx, rhs_sl, rhs_res):
                ops = []
                for dc in range(8):
                    la = la_sb[:, (widx * 8 + dc) * 32:(widx * 8 + dc + 1) * 32]
                    ops.append(mmf(la, rhs_sl(dc)[:, 0:512],
                                   ps_s0[0:32, 0:512], dc == 0, dc == 7))
                    ops.append(mmf(la, rhs_sl(dc)[:, 512:NT],
                                   ps_s0[0:32, 512:NT], dc == 0, dc == 7))
                p.op("tensor", ops, reads=("la",) + rhs_res, writes=("fs0",))
                p.op("scalar", lambda e: e.copy(out=mid_sb[:], in_=ps_s0[0:32, :]),
                     reads=("fs0",), writes=("mid",))

            def qlora_mms(widx, rhs_sl, rhs_res, och, pst, pres):
                slot = widx % 2
                ops = []
                rds = {"mid"} | set(rhs_res)
                for dc in range(8):
                    lh = wbuf[:, slot * 8 * D + dc * D + och * 128:
                              slot * 8 * D + dc * D + och * 128 + 128]
                    rds.add("wb%d_%d" % (slot, dc))
                    ops.append(mmf(lh, rhs_sl(dc)[:, 0:512], pst[:, 0:512], dc == 0, False))
                    ops.append(mmf(lh, rhs_sl(dc)[:, 512:NT], pst[:, 512:NT], dc == 0, False))
                lbs = lb_sb[:, widx * D + och * 128: widx * D + och * 128 + 128]
                ops.append(mmf(lbs, mid_sb[:, 0:512], pst[:, 0:512], False, True))
                ops.append(mmf(lbs, mid_sb[:, 512:NT], pst[:, 512:NT], False, True))
                p.op("tensor", ops, reads=tuple(rds) + ("lb",), writes=(pres,))

            top_ln_to("h")
            if tap == "h0_%d" % call:
                do_tap(hb[:], ("h",))
                return
            (dequant_weight if call == 0 else stream_weight)(0)
            (dequant_weight if call == 0 else stream_weight)(1)
            if tap == "wb_%d" % call:
                do_tap(wbuf[:], tuple("wb%d_%d" % (sl_, dc_) for sl_ in range(2)
                                      for dc_ in range(8)))
                return
            for blk in range(3):
                i0, i1 = 2 * blk, 2 * blk + 1
                for H in range(2):
                    t0 = H * NT
                    hsl = lambda dc, t0=t0: hb[:, dc * T + t0: dc * T + t0 + NT]
                    gsl = lambda dc: gbuf[:, dc * NT:(dc + 1) * NT]
                    gres = tuple("g%d" % i for i in range(8))
                    lora_mid(i0, hsl, ("h",))
                    if tap == "mid_%d_%d_%d" % (call, blk, H):
                        do_tap(mid_sb[:], ("mid",))
                        return

                    def emit_stats_mms(och):
                        ysl = yraw[:, och * NT:(och + 1) * NT]
                        y2c = y2s[och % 2]
                        p.op("tensor",
                             [mmf(onesb_sb[:], ysl[:, 0:512], ps_s0[:, 0:512], och == 0, och == 7),
                              mmf(onesb_sb[:], ysl[:, 512:NT], ps_s0[:, 512:NT], och == 0, och == 7)],
                             reads=("onesb", "yr%d" % och), writes=("fs0",))
                        p.op("tensor",
                             [mmf(onesb_sb[:], y2c[:, 0:512], ps_s1[:, 0:512], och == 0, och == 7),
                              mmf(onesb_sb[:], y2c[:, 512:NT], ps_s1[:, 512:NT], och == 0, och == 7)],
                             reads=("onesb", "y2%d" % (och % 2)), writes=("fs1",))

                    for och in range(8):
                        pr = "fy%d" % (och % 2)
                        qlora_mms(i0, hsl, ("h",), och, ps_y[och % 2], pr)
                        bcol = bc_sb[:, i0 * 8 + och: i0 * 8 + och + 1]
                        ysl = yraw[:, och * NT:(och + 1) * NT]
                        p.op("scalar", lambda e, d=ysl, s=ps_y[och % 2], b=bcol:
                             e.activation(out=d, in_=s[:], func=AF.Identity,
                                          bias=b, scale=1.0),
                             reads=(pr, "bcol"), writes=("yr%d" % och,))
                        p.op("scalar", lambda e, s=ps_y[och % 2], b=bcol,
                             d=y2s[och % 2]:
                             e.activation(out=d[:], in_=s[:], func=AF.Square,
                                          bias=b, scale=1.0),
                             reads=(pr, "bcol"), writes=("y2%d" % (och % 2),))
                        if och >= 1:
                            emit_stats_mms(och - 1)
                    emit_stats_mms(7)
                    if tap == "yraw_%d_%d_%d" % (call, blk, H):
                        do_tap(yraw[:], tuple("yr%d" % i for i in range(8)))
                        return
                    stats_rows(ps_s0[:], ("fs0",), ps_s1[:], ("fs1",), D)
                    if tap == "pre" and not TAP["done"]:
                        do_tap(y2[:], ("y2",))
                        return
                    for och in range(8):
                        ysl = yraw[:, och * NT:(och + 1) * NT]
                        p.op("vector", lambda e, a=ysl: e.tensor_tensor(
                            out=y2[:], in0=a, in1=r_rep[:], op=OP.mult),
                            reads=("yr%d" % och, "rr"), writes=("y2",))
                        if tap == "ap%d" % och and not TAP["done"]:
                            do_tap(y2[:], ("y2",))
                            return
                        p.op("vector", lambda e: e.tensor_tensor(
                            out=y2[:], in0=y2[:], in1=nmr_rep[:], op=OP.add),
                            reads=("y2", "nmr"), writes=("y2",))
                        if tap == "aq%d" % och and not TAP["done"]:
                            do_tap(y2[:], ("y2",))
                            return
                        gcol = blk * 8 + och
                        p.op("scalar", lambda e, d=gsl(och), gc=gcol: e.activation(
                            out=d, in_=y2[:], func=AF.Gelu,
                            bias=lnb_sb[:, gc:gc + 1], scale=lnw_sb[:, gc:gc + 1]),
                            reads=("y2", "lnw", "lnb"), writes=("g%d" % och,))
                        if tap == "ag%d" % och and not TAP["done"]:
                            do_tap(gbuf[:], ("g%d" % och,))
                            return
                    if tap == "g_%d_%d_%d" % (call, blk, H):
                        do_tap(gbuf[:], gres)
                        return
                    lora_mid(i1, gsl, gres)
                    for och in range(8):
                        pr = "fy%d" % (och % 2)
                        qlora_mms(i1, gsl, gres, och, ps_y[och % 2], pr)
                        bcol = bc_sb[:, i1 * 8 + och: i1 * 8 + och + 1]
                        hs = hsl(och)
                        p.op("vector", lambda e, d=hs, s=ps_y[och % 2], b=bcol:
                             e.scalar_tensor_tensor(out=d, in0=s[:], scalar=b,
                                                    in1=d, op0=OP.add, op1=OP.add),
                             reads=(pr, "bcol", "h"), writes=("h",))
                if tap == "hblk_%d_%d" % (call, blk):
                    do_tap(hb[:], ("h",))
                    return
                for wnext in (i0 + 2, i0 + 3):
                    if wnext <= 5:
                        (dequant_weight if call == 0 else stream_weight)(wnext)
            for k in range(16):
                dc, H = k // 2, k % 2
                sl = xstage[:, (k % 2) * NT:(k % 2 + 1) * NT]
                xres = "xs%d" % (k % 2)
                dsl = out_t[dc * 128:(dc + 1) * 128, H * NT:(H + 1) * NT]
                hs = hb[:, dc * T + H * NT: dc * T + H * NT + NT]
                p.dma("sync", sl, dsl, reads=("xd",), writes=(xres,))
                p.op("vector", lambda e, x=sl, h=hs: e.scalar_tensor_tensor(
                    out=x, in0=h, scalar=0.5, in1=x, op0=OP.mult, op1=OP.add),
                    reads=("h", xres), writes=(xres,))
                p.op("vector", lambda e, x=sl, d=xb[:, dc * T + H * NT: dc * T + H * NT + NT]:
                     e.tensor_copy(out=d, in_=x), reads=(xres,), writes=("xb",))
                p.dma("gpsimd", dsl, sl, reads=(xres,), writes=("xd",))
            if call == 0:
                top_ln_to("h")
                if tap == "u":
                    do_tap(hb[:], ("h",))

    def emit_mixer():
        with ExitStack() as mst:
            gbuf = sb(mst, "mgb", [128, 8 * NT], BF16)
            yraw = sb(mst, "myr", [128, T], BF16)
            P1 = psum(mst, "P1", [128, 2048])
            P2 = psum(mst, "P2", [128, 1024])
            P3 = psum(mst, "P3", [128, 512])
            PT = psum(mst, "PT", [128, 1024], BF16)
            p.psum_res |= {"P1q0", "P1q1", "P1q2", "P1q3", "P2", "P3", "PT"}
            P1r = ("P1q0", "P1q1", "P1q2", "P1q3")

            acum = sb(mst, "acum", [32, T])
            combo = sb(mst, "combo", [32, T])
            acumT = sb(mst, "acumT", [128, NCH * 32])
            comboT = None
            vT = sb(mst, "vT", [128, NCH * 32])
            comboT = sb(mst, "comboT", [128, NCH * 32])
            expAT = sb(mst, "expAT", [128, NCH * 32], BF16)
            gamrep = sb(mst, "gamrep", [128, 32])
            ends32 = sb(mst, "ends32", [32, NCH])
            endshi = sb(mst, "endshi", [32, NCH], BF16)
            endslo = sb(mst, "endslo", [32, NCH], BF16)
            bfm = sb(mst, "bfm", [128, T], BF16)
            cfm = sb(mst, "cfm", [128, T], BF16)
            btm = sb(mst, "btm", [128, NCH * 128], BF16)
            rst = sb(mst, "rst", [128, DI])
            rstb = sb(mst, "rstb", [128, DI], BF16)
            xs_sb = sb(mst, "xs_sb", [128, DI], BF16)
            gm_sb = sb(mst, "gm_sb", [128, 128], BF16)
            ygt = sb(mst, "ygt", [128, DI], BF16)
            ccol = sb(mst, "ccol", [128, 8])
            sumc = sb(mst, "sumc", [128, 1])
            sqc = sb(mst, "sqc", [128, 1])
            mucol = sb(mst, "mucol", [128, 1])
            m2col = sb(mst, "m2col", [128, 1])
            varcol = sb(mst, "varcol", [128, 1])
            rcol = sb(mst, "rcol", [128, 1])
            nmrcol = sb(mst, "nmrcol", [128, 1])
            wio = sb(mst, "wio", [128, 2048])
            grow = wio[32:33, 0:NCH * 32]
            wiob = wio[:].bitcast(BF16)
            PTf32 = PT[:].bitcast(F32)
            wob2a = acumT[:].bitcast(BF16)
            wob2b = comboT[:].bitcast(BF16)
            esb = gbuf[:, 4096:4096 + NT].bitcast(BF16)[:, 0:NT] \
                if False else gbuf[:, 4096:4096 + NT]
            esb2 = gbuf[:, 5120:5120 + NT]
            mn8 = gbuf[:, 6144:6144 + NT]
            gdthi = gbuf[0:32, 7168:7200]
            gdtlo = gbuf[0:32, 7200:7232]
            wzb = sb(mst, "wzb", [128, 8 * 512], BF16)
            wxb = sb(mst, "wxb", [128, 2 * 8 * 128], BF16)
            wdtb = sb(mst, "wdtb", [128, 8 * 32], BF16)
            wob = sb(mst, "wob", [128, 16 * 128], BF16)
            prebuf = sb(mst, "prebuf", [128, T + 3], BF16)
            aThi = prebuf[:, 0:NCH * 32]
            aTlo = prebuf[:, NCH * 32:2 * NCH * 32]
            cThi = prebuf[:, 2 * NCH * 32:3 * NCH * 32]
            cTlo = prebuf[:, 3 * NCH * 32:4 * NCH * 32]
            cdg = sb(mst, "cdg", [128, 2 * 4 * 128], BF16)
            cb_sb = sb(mst, "cb_sb", [128, 18])
            nw_sb = sb(mst, "nw_sb", [128, 16])
            nb_sb = sb(mst, "nb_sb", [128, 16])
            nbb_sb = sb(mst, "nbb_sb", [128, 16], BF16)
            dtb_sb = sb(mst, "dtb_sb", [32, 1])
            alog_sb = sb(mst, "alog_sb", [32, 1])
            acol = sb(mst, "acol", [32, 1])

            for dst, src, rn in [(cb_sb[:], cbias, "cb"), (nw_sb[:], nwc, "nw"),
                                 (nb_sb[:], nbc, "nb"), (dtb_sb[:], dtb, "dtbr"),
                                 (alog_sb[:], alog, "alogr")]:
                p.dma("sync", dst, src, writes=(rn,))

            for dc in range(8):
                for cg in range(5):
                    c0 = cg * 1024
                    cw = min(1024, DPROJ - c0)
                    p.dma("sync", wio[:, 0:cw], win[dc * 128:(dc + 1) * 128, c0:c0 + cw],
                          writes=("wio",))
                    p.op("vector", lambda e, w=cw: e.tensor_copy(
                        out=gbuf[:, 0:w], in_=wio[:, 0:w]),
                        reads=("wio",), writes=("gsc",))
                    p.dma("gpsimd", winb_d[dc * 128:(dc + 1) * 128, c0:c0 + cw],
                          gbuf[:, 0:cw], reads=("gsc",), writes=("winb",))
            for dc in range(16):
                p.dma("sync", wio[:, 0:D], wout[dc * 128:(dc + 1) * 128, :],
                      writes=("wio",))
                p.op("vector", lambda e, c=nw_sb[:, dc:dc + 1]: e.tensor_scalar(
                    out=gbuf[:, 0:D], in0=wio[:, 0:D], scalar1=c, scalar2=None,
                    op0=OP.mult), reads=("wio", "nw"), writes=("gsc",))
                p.dma("gpsimd", woutb_d[dc * 128:(dc + 1) * 128, :], gbuf[:, 0:D],
                      reads=("gsc",), writes=("woutbd",))

            p.dma("scalar", wdtb[:].rearrange("p (g n) -> p g n", n=32),
                  winb_d.rearrange("(g p) n -> p g n", p=128)[:, :, 4352:4384],
                  reads=("winb",), writes=("wdtb",))
            if True:
                gf32 = gbuf[:].bitcast(F32)
                dte = gf32[0:32, 0:T]
                lndt = gf32[0:32, T:2 * T]
                for half in range(2):
                    ops = []
                    for dc in range(8):
                        for s4 in range(2):
                            ops.append(mmf(
                                wdtb[:, dc * 32:(dc + 1) * 32],
                                hb[:, dc * T + half * NT + s4 * 512:
                                   dc * T + half * NT + s4 * 512 + 512],
                                P2[0:32, s4 * 512:(s4 + 1) * 512], dc == 0, dc == 7))
                    p.op("tensor", ops, reads=("wdtb", "h"), writes=("P2",))
                    p.op("scalar", lambda e, h=half: e.copy(
                        out=acum[:, h * NT:(h + 1) * NT], in_=P2[0:32, :]),
                        reads=("P2",), writes=("acum",))
                p.op("scalar", lambda e: e.activation(
                    out=acol[:], in_=alog_sb[:], func=AF.Exp),
                    reads=("alogr",), writes=("acol",))
                p.op("vector", lambda e: e.tensor_scalar(
                    out=acol[:], in0=acol[:], scalar1=-1.0, scalar2=None, op0=OP.mult),
                    reads=("acol",), writes=("acol",))
                p.op("scalar", lambda e: e.activation(
                    out=dte[:], in_=acum[:], func=AF.Exp, bias=dtb_sb[:], scale=1.0),
                    reads=("acum", "dtbr", "gsc"), writes=("gsc",))
                p.op("scalar", lambda e: e.activation(
                    out=dte[:], in_=dte[:], func=AF.Ln, bias=1.0, scale=1.0),
                    reads=("gsc",), writes=("gsc",))
                p.op("scalar", lambda e: e.activation(
                    out=lndt[:], in_=dte[:], func=AF.Ln),
                    reads=("gsc",), writes=("gsc",))
                p.op("vector", lambda e: e.tensor_scalar(
                    out=combo[:], in0=dte[:], scalar1=acol[:], scalar2=None,
                    op0=OP.mult), reads=("gsc", "acol"), writes=("combo",))
                for c in range(NCH):
                    p.op("vector", lambda e, c=c: e.tensor_tensor_scan(
                        out=acum[:, c * CS:(c + 1) * CS],
                        data0=onesf_sb[0:32, 0:CS],
                        data1=combo[:, c * CS:(c + 1) * CS], initial=0.0,
                        op0=OP.mult, op1=OP.add),
                        reads=("combo", "onesf"), writes=("acum",))
                p.op("vector", lambda e: e.tensor_tensor(
                    out=combo[:], in0=lndt[:], in1=acum[:], op=OP.subtract),
                    reads=("gsc", "acum"), writes=("combo",))
                for c in range(NCH):
                    p.op("vector", lambda e, c=c: e.tensor_scalar(
                        out=lndt[:, c * CS:(c + 1) * CS],
                        in0=combo[:, c * CS:(c + 1) * CS],
                        scalar1=acum[:, c * CS + CS - 1: c * CS + CS],
                        scalar2=None, op0=OP.add),
                        reads=("combo", "acum", "gsc"), writes=("gsc",))
                p.op("scalar", lambda e: e.activation(
                    out=lndt[:], in_=lndt[:], func=AF.Exp),
                    reads=("gsc",), writes=("gsc",))
                ttmp = wio[64:96, 0:2048]
                for nmres, src, sres, dst in [("aT", acum, "acum", acumT),
                                              ("cT", combo, "combo", comboT),
                                              ("vTT", lndt, "gsc", vT)]:
                    p.op("vector", lambda e, srcl=src: e.transpose(
                        out=ttmp, in_=srcl[:]), reads=(sres, "wio"),
                        writes=("wio",))
                    for w in range(4):
                        p.dma("sync",
                              dst[w * 32:(w + 1) * 32, :]
                              .rearrange("p (c h) -> p c h", h=32),
                              ttmp.rearrange("p (c q) -> p c q", q=128)
                              [:, :, w * 32:w * 32 + 32],
                              reads=("wio",), writes=(nmres,))
                if tap == "aTT":
                    do_tap(acumT[:], ("aT",))
                p.op("scalar", lambda e: e.activation(
                    out=expAT[:], in_=acumT[:], func=AF.Exp),
                    reads=("aT",), writes=("expAT",))
            p.op("vector", lambda e: e.tensor_copy(
                out=ends32[:].unsqueeze(2),
                in_=acum[:].rearrange("p (c q) -> p c q", q=CS)[:, :, CS - 1:CS]),
                reads=("acum",), writes=("ends",))
            p.op("vector", lambda e: e.tensor_copy(out=endshi[:], in_=ends32[:]),
                 reads=("ends",), writes=("ehi",))
            p.op("vector", lambda e: e.tensor_copy(
                out=wio[0:32, 512:512 + NCH], in_=endshi[:]),
                reads=("ehi",), writes=("wio",))
            p.op("vector", lambda e: e.tensor_tensor(
                out=endslo[:], in0=ends32[:], in1=wio[0:32, 512:512 + NCH],
                op=OP.subtract), reads=("ends", "wio"), writes=("elo",))
            p.op("vector", lambda e: e.tensor_scalar(
                out=mn8[:].rearrange("p (j l) -> p j l", l=CS),
                in0=tri_sb[:].unsqueeze(1).to_broadcast([128, 8, CS]),
                scalar1=1.0, scalar2=30000.0, op0=OP.subtract, op1=OP.mult),
                reads=("tri", "gsc"), writes=("mn8",))
            if tap == "dt":
                do_tap(acum[:], ("acum",))
                return

            winb_r = winb_d.rearrange("(g p) n -> p g n", p=128)
            for zg in range(4):
                p.dma("scalar", wzb[:].rearrange("p (g n) -> p g n", n=512),
                      winb_r[:, :, zg * 512:(zg + 1) * 512],
                      reads=("winb",), writes=("wzb",))
                for tc in range(16):
                    q = tc % 4
                    ops = []
                    for dc in range(8):
                        lh = hb[:, dc * T + tc * 128: dc * T + tc * 128 + 128]
                        ops.append(mmf(lh, wzb[:, dc * 512:(dc + 1) * 512],
                                       P1[:, q * 512:(q + 1) * 512],
                                       dc == 0, dc == 7))
                    p.op("tensor", ops, reads=("wzb", "h"), writes=(P1r[q],))
                    p.op("scalar", lambda e, q=q: e.activation(
                        out=gbuf[:, q * 512:(q + 1) * 512],
                        in_=P1[:, q * 512:(q + 1) * 512],
                        func=AF.Silu), reads=(P1r[q],), writes=("gsc",))
                    p.dma("gpsimd",
                          zsil_d[tc * 128:(tc + 1) * 128, zg * 512:(zg + 1) * 512],
                          gbuf[:, q * 512:(q + 1) * 512],
                          reads=("gsc",), writes=("zsil",))

            for och in range(18):
                slot = och % 2
                p.dma("scalar",
                      wxb[:, slot * 1024: slot * 1024 + 1024]
                      .rearrange("p (g n) -> p g n", n=128),
                      winb_r[:, :, 2048 + och * 128: 2048 + (och + 1) * 128],
                      reads=("winb",), writes=("wxb%d" % slot,))
                ops = []
                for dc in range(8):
                    lh = wxb[:, slot * 1024 + dc * 128: slot * 1024 + (dc + 1) * 128]
                    for s4 in range(4):
                        ops.append(mmf(lh, hb[:, dc * T + s4 * 512: dc * T + (s4 + 1) * 512],
                                       P1[:, s4 * 512:(s4 + 1) * 512], dc == 0, dc == 7))
                p.op("tensor", ops, reads=("wxb%d" % slot, "h"), writes=P1r)
                p.op("vector", lambda e: e.tensor_copy(out=yraw[:, 0:T], in_=P1[:]),
                     reads=P1r, writes=("ysc",))
                p.dma("gpsimd", xbcpre_d[och * 128:(och + 1) * 128, :], yraw[:, 0:T],
                      reads=("ysc",), writes=("xbcpre",))
            if tap == "inproj":
                do_tap(hb[:], ("h",))
                return

            for och in range(18):
                p.dma("sync", prebuf[:, 3:3 + T],
                      xbcpre_d[och * 128:(och + 1) * 128, :],
                      reads=("xbcpre",), writes=("prebuf",))
                p.op("vector", lambda e: e.memset(ap=prebuf[:, 0:3], constant=0.0),
                     writes=("prebuf",))
                slot = och % 2
                p.dma("scalar", wio[:, 0:512].rearrange("p (i q) -> p i q", q=128),
                      cdiag.rearrange("p (i c q) -> p c i q", c=18, q=128)[:, och, :, :],
                      reads=(), writes=("wio",))
                p.op("vector", lambda e, s=slot: e.tensor_copy(
                    out=cdg[:, s * 512:(s + 1) * 512], in_=wio[:, 0:512]),
                    reads=("wio",), writes=("cdg%d" % slot,))
                ops = []
                for s4 in range(4):
                    for i in range(4):
                        ops.append(mmf(cdg[:, slot * 512 + i * 128: slot * 512 + (i + 1) * 128],
                                       prebuf[:, s4 * 512 + i: s4 * 512 + i + 512],
                                       P1[:, s4 * 512:(s4 + 1) * 512], i == 0, i == 3))
                p.op("tensor", ops, reads=("cdg%d" % slot, "prebuf"), writes=P1r)
                dsttile = bfm if och == 16 else (cfm if och == 17 else prebuf[:, 3:3 + T])
                dres = "bfm" if och == 16 else ("cfm" if och == 17 else "prebuf")
                p.op("scalar", lambda e, d=dsttile, c=cb_sb[:, och:och + 1]:
                     e.activation(out=d[:], in_=P1[:], func=AF.Silu, bias=c, scale=1.0),
                     reads=P1r + ("cb",), writes=(dres,))
                if och < 17:
                    for grp in range(2):
                        ops = []
                        for k in range(8):
                            tc = grp * 8 + k
                            ops.append(lambda e, s=dsttile[:, tc * 128:(tc + 1) * 128],
                                       o=PT[:, k * 128:(k + 1) * 128]:
                                       e.transpose(out=o, in_=s, identity=idb_sb[:]))
                        p.op("tensor", ops, reads=(dres, "idb"), writes=("PT",))
                        if och < 16:
                            p.op("vector", lambda e: e.tensor_copy(
                                out=yraw[:, 0:1024], in_=PT[:]),
                                reads=("PT",), writes=("ysc",))
                            for k in range(8):
                                tc = grp * 8 + k
                                p.dma("gpsimd",
                                      xtm_d[tc * 128:(tc + 1) * 128,
                                            och * 128:(och + 1) * 128],
                                      yraw[:, k * 128:(k + 1) * 128],
                                      reads=("ysc",), writes=("xtm%d" % tc,))
                        else:
                            p.op("vector",
                                 lambda e, d=btm[:, grp * 1024:(grp + 1) * 1024]:
                                 e.tensor_copy(out=d, in_=PT[:]),
                                 reads=("PT",), writes=("btm",))
            if tap == "conv":
                do_tap(cfm[:], ("cfm",))
                return


            p.op("vector", lambda e: e.tensor_copy(out=nbb_sb[:], in_=nb_sb[:]),
                 reads=("nb",), writes=("nbb",))
            for dcg in range(16):
                slot = dcg % 2
                p.dma("scalar", wzb[:, slot * D: slot * D + D],
                      woutb_d[dcg * 128:(dcg + 1) * 128, :],
                      reads=("woutbd",), writes=("wzb",))
                ops = [mmf(wzb[:, slot * D + och * 128: slot * D + och * 128 + 128],
                           nbb_sb[:, dcg:dcg + 1], P3[:, och:och + 1],
                           dcg == 0 and och == 0, dcg == 15 and och == 7)
                       for och in range(8)]
                p.op("tensor", ops, reads=("wzb", "nbb"), writes=("P3",))
            p.op("vector", lambda e: e.tensor_copy(out=ccol[:], in_=P3[:, 0:8]),
                 reads=("P3",), writes=("ccol",))

            first = ["cfm"]
            for src, sres, dhi, dlo, rhi, rlo in [
                    (acumT, "aT", aThi, aTlo, "aThi", "aTlo"),
                    (comboT, "cT", cThi, cTlo, "cThi", "cTlo")]:
                p.op("vector", lambda e, s=src, d=dhi: e.tensor_copy(
                    out=d, in_=s[:]), reads=(sres,) + tuple(first),
                    writes=(rhi,))
                first = []
                p.op("vector", lambda e, d=dhi: e.tensor_copy(
                    out=wio[:, 0:NCH * 32], in_=d),
                    reads=(rhi,), writes=("wio",))
                p.op("vector", lambda e, s=src, d=dlo: e.tensor_tensor(
                    out=d, in0=s[:], in1=wio[:, 0:NCH * 32],
                    op=OP.subtract), reads=(sres, "wio"), writes=(rlo,))

            if tap == "tg":
                do_tap(acumT[:], ("wio",))
            with ExitStack() as sst:
                xtmc = sb(sst, "xtmc", [128, DI], BF16)
                zslc = sb(sst, "zslc", [128, DI], BF16)
                p.op("vector", lambda e: e.memset(ap=rst[:], constant=0.0),
                     writes=("rst",))
                p.op("vector", lambda e: e.memset(ap=rstb[:], constant=0.0),
                     writes=("rstb",))
                for c in range(NCH):
                    cl = slice(c * CS, (c + 1) * CS)
                    sl = 0
                    xtms = xtmc[:]
                    p.dma("sync", xtms, xtm_d[cl, :],
                          reads=("xtm%d" % c,), writes=("xtmc%d" % sl,))
                    p.dma("sync", zslc[:], zsil_d[cl, :],
                          reads=("zsil",), writes=("zslc%d" % sl,))
                    for s2 in range(2):
                        ops = [mmf(cfm[:, cl],
                                   rstb[:, s2 * 1024 + s4 * 512: s2 * 1024 + (s4 + 1) * 512],
                                   P2[:, s4 * 512:(s4 + 1) * 512], True, True)
                               for s4 in range(2)]
                        p.op("tensor", ops, reads=("cfm", "rstb"), writes=("P2",))
                        p.op("vector", lambda e, c=c, s2=s2: e.tensor_tensor(
                            out=xs_sb[:, s2 * 1024:(s2 + 1) * 1024]
                            .rearrange("p (h q) -> p h q", q=64),
                            in0=P2[:].rearrange("p (h q) -> p h q", q=64),
                            in1=expAT[:, c * 32 + s2 * 16: c * 32 + (s2 + 1) * 16]
                            .unsqueeze(2).to_broadcast([128, 16, 64]), op=OP.mult),
                            reads=("P2", "expAT"), writes=("xs",))
                    if tap == "tyo%d" % c:
                        do_tap(xs_sb[:], ("xs",))
                    p.op("tensor", mmf(bfm[:, cl], cfm[:, cl], P3[:, 0:128], True, True),
                         reads=("bfm", "cfm"), writes=("P3",))
                    p.op("vector", lambda e: e.tensor_tensor(
                        out=gm_sb[:], in0=P3[:, 0:128], in1=tri_sb[:], op=OP.mult),
                        reads=("P3", "tri"), writes=("gm",))
                    if tap == "tGm%d" % c:
                        do_tap(gm_sb[:], ("gm",))
                    for hf in range(4):
                        par = hf % 2
                        slot = wiob[:, par * 2048: par * 2048 + 2048]
                        sres = "wio" if par == 0 else "wioB"
                        esbX = esb if par == 0 else esb2
                        eres = "gse" if par == 0 else "gse2"
                        a8 = slice(c * 32 + hf * 8, c * 32 + hf * 8 + 8)
                        p.op("vector", lambda e, s=slot, a=a8: e.tensor_tensor(
                            out=s[:, 0:1024].rearrange("p (j l) -> p j l", l=CS),
                            in0=idb_sb[:].unsqueeze(1).to_broadcast([128, 8, CS]),
                            in1=aThi[:, a].unsqueeze(2).to_broadcast([128, 8, CS]),
                            op=OP.mult), reads=("idb", "aThi"), writes=(sres,))
                        p.op("vector", lambda e, s=slot, a=a8: e.tensor_tensor(
                            out=s[:, 1024:2048].rearrange("p (j l) -> p j l", l=CS),
                            in0=idb_sb[:].unsqueeze(1).to_broadcast([128, 8, CS]),
                            in1=aTlo[:, a].unsqueeze(2).to_broadcast([128, 8, CS]),
                            op=OP.mult), reads=("idb", "aTlo"), writes=(sres,))
                        for half in range(2):
                            bank = P3[:, 0:512] if half == 0 else PTf32[:, 0:512]
                            bres = "P3" if half == 0 else "PT"
                            c4 = slice(c * 32 + hf * 8 + half * 4,
                                       c * 32 + hf * 8 + half * 4 + 4)
                            h0 = half * 512
                            ops = [
                                mmf(onesb_sb[:], slot[:, h0:h0 + 512],
                                    bank, True, False),
                                mmf(onesb_sb[:], slot[:, 1024 + h0:1024 + h0 + 512],
                                    bank, False, False),
                                mmf(idb_sb[:], cThi[:, c4].unsqueeze(2)
                                    .to_broadcast([128, 4, CS]), bank, False, False),
                                mmf(idb_sb[:], cTlo[:, c4].unsqueeze(2)
                                    .to_broadcast([128, 4, CS]), bank, False, False),
                                mmf(idb_sb[:], mn8[:, h0:h0 + 512],
                                    bank, False, True)]
                            p.op("tensor", ops,
                                 reads=(sres, "cThi", "cTlo", "mn8", "onesb", "idb"),
                                 writes=(bres,))
                            p.op("scalar", lambda e, b=bank,
                                 o=esbX[:, h0:h0 + 512]:
                                 e.activation(out=o, in_=b, func=AF.Exp),
                                 reads=(bres,), writes=(eres,))
                        if tap == "tE%d_%d" % (c, hf):
                            do_tap(esbX[:], (eres,))
                        p.op("vector", lambda e, s=esbX: e.tensor_tensor(
                            out=s[:].rearrange("p (h l) -> p h l", l=128),
                            in0=s[:].rearrange("p (h l) -> p h l", l=128),
                            in1=gm_sb[:].unsqueeze(1).to_broadcast([128, 8, 128]),
                            op=OP.mult), reads=(eres, "gm"), writes=(eres,))
                        ops = []
                        for k in range(8):
                            h = hf * 8 + k
                            ops.append(mmf(esbX[:, k * 128:(k + 1) * 128],
                                           xtms[:, h * 64:(h + 1) * 64],
                                           P1[:, h * 64:(h + 1) * 64],
                                           k == 0, k == 7))
                        p.op("tensor", ops, reads=(eres, "xtmc%d" % sl),
                             writes=(P1r[hf],))
                    p.op("vector", lambda e: e.tensor_tensor(
                        out=ygt[:], in0=P1[:], in1=xs_sb[:], op=OP.add),
                        reads=P1r + ("xs",), writes=("ygt",))
                    p.op("vector", lambda e, z=zslc[:]: e.tensor_tensor(
                        out=ygt[:], in0=ygt[:], in1=z, op=OP.mult),
                        reads=("ygt", "zslc%d" % sl), writes=("ygt",))
                    if tap == "yg%d" % c:
                        do_tap(ygt[:], ("ygt",))
                    if tap == "yo%d" % c:
                        do_tap(xs_sb[:], ("xs",))
                    p.op("vector", lambda e: e.tensor_scalar(
                        out=yraw[:, 0:DI], in0=ygt[:], scalar1=1.0, scalar2=None,
                        op0=OP.mult, op1=OP.add, accum_out=sumc[:]),
                        reads=("ygt",), writes=("sumc", "ysc"))
                    p.op("scalar", lambda e: e.activation(
                        out=yraw[:, 0:DI], in_=ygt[:], func=AF.Square),
                        reads=("ygt",), writes=("ysc",))
                    p.op("vector", lambda e: e.tensor_scalar(
                        out=yraw[:, 0:DI], in0=yraw[:, 0:DI], scalar1=1.0,
                        scalar2=None, op0=OP.mult, op1=OP.add, accum_out=sqc[:]),
                        reads=("ysc",), writes=("sqc", "ysc"))
                    p.op("vector", lambda e: e.tensor_scalar(
                        out=mucol[:], in0=sumc[:], scalar1=-1.0 / DI, scalar2=None,
                        op0=OP.mult), reads=("sumc",), writes=("mucol",))
                    p.op("scalar", lambda e: e.activation(
                        out=m2col[:], in_=sumc[:], func=AF.Square, scale=1.0 / DI),
                        reads=("sumc",), writes=("m2col",))
                    p.op("vector", lambda e: e.tensor_scalar(
                        out=varcol[:], in0=sqc[:], scalar1=1.0 / DI, scalar2=float(EPS),
                        op0=OP.mult, op1=OP.add), reads=("sqc",), writes=("varcol",))
                    p.op("vector", lambda e: e.tensor_tensor(
                        out=varcol[:], in0=varcol[:], in1=m2col[:], op=OP.subtract),
                        reads=("varcol", "m2col"), writes=("varcol",))
                    p.op("scalar", lambda e: e.activation(
                        out=varcol[:], in_=varcol[:], func=AF.Ln),
                        reads=("varcol",), writes=("varcol",))
                    p.op("scalar", lambda e: e.activation(
                        out=rcol[:], in_=varcol[:], func=AF.Exp, scale=-0.5),
                        reads=("varcol",), writes=("rcol",))
                    p.op("vector", lambda e: e.tensor_tensor(
                        out=nmrcol[:], in0=mucol[:], in1=rcol[:], op=OP.mult),
                        reads=("mucol", "rcol"), writes=("nmrcol",))
                    p.op("vector", lambda e: e.tensor_scalar(
                        out=ygt[:], in0=ygt[:], scalar1=rcol[:], scalar2=nmrcol[:],
                        op0=OP.mult, op1=OP.add),
                        reads=("ygt", "rcol", "nmrcol"), writes=("ygt",))
                    if tap == "yln%d" % c:
                        do_tap(ygt[:], ("ygt",))
                        return
                    for grp in range(2):
                        ops = []
                        for k in range(8):
                            dcx = grp * 8 + k
                            ops.append(lambda e, s=ygt[:, dcx * 128:(dcx + 1) * 128],
                                       o=PT[:, k * 128:(k + 1) * 128]:
                                       e.transpose(out=o, in_=s, identity=idb_sb[:]))
                        p.op("tensor", ops, reads=("ygt", "idb"), writes=("PT",))
                        p.op("vector", lambda e, c=c, grp=grp: e.tensor_copy(
                            out=gbuf[:, 0:4096].rearrange("p (d q) -> p d q", q=256)
                            [:, grp * 8:(grp + 1) * 8, (c % 2) * 128:(c % 2) * 128 + 128],
                            in_=PT[:].rearrange("p (d q) -> p d q", q=128)),
                            reads=("PT",), writes=("gsc",))
                    p.op("vector", lambda e, c=c: e.tensor_tensor(
                        out=xs_sb[:].rearrange("p (h q) -> p h q", q=64),
                        in0=xtms.rearrange("p (h q) -> p h q", q=64),
                        in1=vT[:, c * 32:(c + 1) * 32].unsqueeze(2)
                        .to_broadcast([128, 32, 64]), op=OP.mult),
                        reads=("xtmc%d" % sl, "vTT"), writes=("xs",))
                    p.op("vector", lambda e, c=c: e.tensor_tensor(
                        out=gdthi, in0=idb_sb[0:32, 0:32],
                        in1=endshi[:, c:c + 1].to_broadcast([32, 32]),
                        op=OP.mult), reads=("idb", "ehi"), writes=("gdt",))
                    p.op("vector", lambda e, c=c: e.tensor_tensor(
                        out=gdtlo, in0=idb_sb[0:32, 0:32],
                        in1=endslo[:, c:c + 1].to_broadcast([32, 32]),
                        op=OP.mult), reads=("idb", "elo"), writes=("gdt",))
                    p.op("tensor",
                         [mmf(onesb_sb[0:32, :], gdthi, P3[:, 128:160],
                              True, False),
                          mmf(onesb_sb[0:32, :], gdtlo, P3[:, 128:160],
                              False, True)],
                         reads=("gdt", "onesb"), writes=("P3",))
                    p.op("scalar", lambda e: e.activation(
                        out=gamrep[:], in_=P3[:, 128:160], func=AF.Exp),
                        reads=("P3",), writes=("gamrep",))
                    if tap == "tgam%d" % c:
                        do_tap(gamrep[:], ("gamrep",))
                    p.op("gpsimd", lambda e: e.tensor_tensor(
                        out=rst[:].rearrange("p (h q) -> p h q", q=64),
                        in0=rst[:].rearrange("p (h q) -> p h q", q=64),
                        in1=gamrep[:].unsqueeze(2).to_broadcast([128, 32, 64]),
                        op=OP.mult), reads=("rst", "gamrep"), writes=("rst",))
                    for s2 in range(2):
                        ops = [mmf(btm[:, c * 128:(c + 1) * 128],
                                   xs_sb[:, s2 * 1024 + s4 * 512:
                                         s2 * 1024 + (s4 + 1) * 512],
                                   P2[:, s4 * 512:(s4 + 1) * 512], True, True)
                               for s4 in range(2)]
                        p.op("tensor", ops, reads=("btm", "xs"), writes=("P2",))
                        p.op("vector", lambda e, s2=s2: e.tensor_tensor(
                            out=rst[:, s2 * 1024:(s2 + 1) * 1024],
                            in0=rst[:, s2 * 1024:(s2 + 1) * 1024],
                            in1=P2[:], op=OP.add),
                            reads=("P2", "rst"), writes=("rst",))
                    p.op("vector", lambda e: e.tensor_copy(out=rstb[:], in_=rst[:]),
                         reads=("rst",), writes=("rstb",))
                    if c % 2 == 1:
                        gidx = c // 2
                        wsrc = woutb_d.rearrange("(g p) n -> p g n", p=128)
                        for och in range(8):
                            q = och % 4
                            osl = slice(och * 128, (och + 1) * 128)
                            if och % 2 == 0:
                                p.dma("scalar",
                                      wob[:].rearrange("p (g n) -> p g n", n=128),
                                      wsrc[:, :, osl],
                                      reads=("woutbd",), writes=("wo0",))
                                wt = [wob[:, dcg * 128:(dcg + 1) * 128]
                                      for dcg in range(16)]
                                rds = ("wo0",)
                            else:
                                rex = ("aTlo", "cTlo", "aThi", "cThi") \
                                    if c == 1 and och == 1 else ()
                                p.dma("scalar",
                                      wob2a.rearrange("p (g n) -> p g n", n=128),
                                      wsrc[:, 0:8, osl],
                                      reads=("woutbd",) + rex, writes=("wo1",))
                                p.dma("scalar",
                                      wob2b.rearrange("p (g n) -> p g n", n=128),
                                      wsrc[:, 8:16, osl],
                                      reads=("woutbd",), writes=("wo1",))
                                wt = [(wob2a if dcg < 8 else wob2b)
                                      [:, (dcg % 8) * 128:(dcg % 8) * 128 + 128]
                                      for dcg in range(16)]
                                rds = ("wo1",)
                            ops = [mmf(wt[dcg],
                                       gbuf[:, dcg * 256:(dcg + 1) * 256],
                                       P1[:, q * 512:q * 512 + 256],
                                       dcg == 0, dcg == 15)
                                   for dcg in range(16)]
                            p.op("tensor", ops, reads=rds + ("gsc",),
                                 writes=(P1r[q],))
                            ssl = xstage[:, (och % 2) * NT: (och % 2) * NT + 256]
                            xres = "xs%d" % (och % 2)
                            dsl = out_t[och * 128:(och + 1) * 128,
                                        gidx * 256:(gidx + 1) * 256]
                            p.dma("sync", ssl, dsl, reads=("xd",), writes=(xres,))
                            p.op("vector", lambda e, x=ssl, q=q, o=och:
                                 e.scalar_tensor_tensor(
                                     out=x, in0=P1[:, q * 512:q * 512 + 256],
                                     scalar=ccol[:, o:o + 1], in1=x,
                                     op0=OP.add, op1=OP.add),
                                 reads=(P1r[q], "ccol", xres), writes=(xres,))
                            p.dma("gpsimd", dsl, ssl, reads=(xres,), writes=("xd",))
            for k in range(16):
                dc, H = k // 2, k % 2
                ssl = xstage[:, (k % 2) * NT:(k % 2 + 1) * NT]
                xres = "xs%d" % (k % 2)
                p.dma("sync", ssl, out_t[dc * 128:(dc + 1) * 128, H * NT:(H + 1) * NT],
                      reads=("xd",), writes=(xres,))
                p.op("vector", lambda e, x=ssl,
                     d=xb[:, dc * T + H * NT: dc * T + H * NT + NT]:
                     e.tensor_copy(out=d, in_=x), reads=(xres,), writes=("xb",))
            if tap == "xmix":
                do_tap(xb[:], ("xb",))
                return

    emit_ffn(0)
    if not p.dead:
        p.barrier()
    emit_mixer()
    if not p.dead:
        p.barrier()
    emit_ffn(1)
    p.final_wait("gpsimd", ("xd",))
    p.op("gpsimd", lambda e: e.nop())
    p.realize()
    st.close()
    return nc


def host_inputs(inputs):
    u = np.asarray(inputs["L0_attn_u"])
    w_in = np.ascontiguousarray(inputs["L0_attn_W_in"], np.float32)
    conv_w = np.asarray(inputs["L0_attn_conv_w"])
    conv_b = np.asarray(inputs["L0_attn_conv_b"])
    dt_bias = np.asarray(inputs["L0_attn_dt_bias"]).reshape(32, 1).astype(np.float32)
    a_log = np.asarray(inputs["L0_attn_A_log"]).reshape(32, 1).astype(np.float32)
    norm_w = np.asarray(inputs["L0_attn_norm_w"])
    norm_b = np.asarray(inputs["L0_attn_norm_b"])
    w_out = np.ascontiguousarray(inputs["L0_attn_W_out"], np.float32)
    qw = np.asarray(inputs["L0_ffn_qweights"])
    sc = np.asarray(inputs["L0_ffn_scales"])
    bi = np.asarray(inputs["L0_ffn_biases"], np.float32)
    la = np.asarray(inputs["L0_ffn_lora_a"])
    lb = np.asarray(inputs["L0_ffn_lora_b"])
    lnw = np.asarray(inputs["L0_ffn_ln_gamma"])
    lnb = np.asarray(inputs["L0_ffn_ln_beta"])

    wqt = np.ascontiguousarray(qw.transpose(0, 2, 1), np.int32)
    srt = np.ascontiguousarray(np.repeat(sc, 16, axis=2).transpose(0, 2, 1),
                               np.float32)
    biasc = np.ascontiguousarray(
        bi.reshape(6, 8, 128).transpose(2, 0, 1).reshape(128, 48), np.float32)
    lat = np.ascontiguousarray(
        la.transpose(0, 2, 1).reshape(6, 8, 128, 32).transpose(2, 0, 1, 3)
        .reshape(128, 6 * 8 * 32), np.float32)
    lbt = np.ascontiguousarray(
        lb.transpose(0, 2, 1).transpose(1, 0, 2).reshape(32, 6 * D), np.float32)
    lnw3 = np.ascontiguousarray(
        lnw.reshape(3, 8, 128).transpose(2, 0, 1).reshape(128, 24), np.float32)
    lnb3 = np.ascontiguousarray(
        lnb.reshape(3, 8, 128).transpose(2, 0, 1).reshape(128, 24), np.float32)
    nwc = np.ascontiguousarray(norm_w.reshape(16, 128).T, np.float32)
    nbc = np.ascontiguousarray(norm_b.reshape(16, 128).T, np.float32)
    cb = np.ascontiguousarray(conv_b.reshape(18, 128).T, np.float32)
    cd = np.zeros((128, 4 * 18 * 128), np.float32)
    for i in range(4):
        for c in range(18):
            blk = cd[:, (i * 18 + c) * 128:(i * 18 + c + 1) * 128]
            np.fill_diagonal(blk, conv_w[c * 128:(c + 1) * 128, i])
    identb = np.eye(128, dtype=ml_dtypes.bfloat16)
    identf = np.eye(128, dtype=np.float32)
    tri = np.triu(np.ones((128, 128), np.float32)).copy()
    onesbf = np.ones((128, 128), dtype=ml_dtypes.bfloat16)
    onesf = np.ones((128, 512), np.float32)

    shared = dict(win=w_in, wqt=wqt, srt=srt, biasc=biasc, lat=lat, lbt=lbt,
                  lnw3=lnw3, lnb3=lnb3, wout=w_out, nwc=nwc, nbc=nbc,
                  cdiag=cd, cbias=cb, dtb=dt_bias, alog=a_log,
                  identb=identb, identf=identf, trimask=tri,
                  onesbf=onesbf, onesf=onesf)
    in_maps = []
    for b in range(8):
        m = dict(shared)
        m["xin"] = np.ascontiguousarray(u[b].T, np.float32)
        in_maps.append(m)
    return in_maps


def kernel(**inputs):
    in_maps = host_inputs(inputs)
    nc = build_program()
    res = run_bass_kernel_spmd(nc, in_maps, core_ids=list(range(8)))
    out = np.stack([np.asarray(r["out"]).T for r in res.results])
    return out.astype(np.float32)


# version 14
# speedup vs baseline: 1.1566x; 1.1566x over previous
import numpy as np
import ml_dtypes
import concourse.bass as bass
import concourse.mybir as mybir
from concourse.bass_utils import run_bass_kernel_spmd
from concourse.alu_op_type import AluOpType
from contextlib import ExitStack

F32 = mybir.dt.float32
BF16 = mybir.dt.bfloat16
I32 = mybir.dt.int32
AF = mybir.ActivationFunctionType
OP = AluOpType

T = 2048
D = 1024
DI = 2048
DCONV = 2304
DPROJ = 4384
CS = 128
NCH = T // CS
NT = 1024
EPS = 1e-5

ENGS = ("sync", "vector", "scalar", "tensor", "gpsimd")
DMA_SEM = {"sync": "dSY", "vector": "dV", "scalar": "dSC", "tensor": "dT",
           "gpsimd": "dG"}


class Prog:

    def __init__(self, nc, st):
        self.nc = nc
        self.st = st
        self.items = {e: [] for e in ENGS}
        self.semn = {}
        self.semh = {}
        for e in ENGS:
            self.semh[e] = st.enter_context(nc.semaphore("s_" + e))
            self.semn[e] = 0
        self.res = {}
        self.waited = {e: {} for e in ENGS}
        self.nops = 0
        self.nsem = len(ENGS)
        self.dead = False
        self.psum_res = set()

    SEM_GROUP = {}

    def dma_sem(self, eng, res):
        res = self.SEM_GROUP.get(res, res)
        nm = "d|%s|%s" % (eng, res)
        if nm not in self.semh:
            self.semh[nm] = self.st.enter_context(
                self.nc.semaphore("d_%s_%s" % (eng, res)))
            self.semn[nm] = 0
            self.nsem += 1
        return nm

    def _r(self, name):
        return self.res.setdefault(name, {"w": None, "r": []})

    def _need(self, reads, writes):
        needed = {}

        def add(sn, tk):
            if sn.startswith("d|"):
                tk = self.semn[sn]
            needed[sn] = max(needed.get(sn, 0), tk)

        for r in reads:
            w = self._r(r)["w"]
            if w is not None:
                add(*w)
        for r in writes:
            rec = self._r(r)
            if rec["w"] is not None:
                add(*rec["w"])
            for (sn, tk) in rec["r"]:
                add(sn, tk)
        return needed

    def _waits(self, eng, needed):
        wt = self.waited[eng]
        for sn, tk in sorted(needed.items()):
            if wt.get(sn, 0) >= tk:
                continue
            wt[sn] = tk
            h = self.semh[sn]
            self.items[eng].append(lambda e, h=h, v=tk: e.wait_ge(h, v))

    def _update(self, reads, writes, ident):
        for r in reads:
            self._r(r)["r"].append(ident)
        for r in writes:
            rec = self._r(r)
            rec["w"] = ident
            rec["r"] = []

    def op(self, eng, fns, reads=(), writes=()):
        if self.dead:
            return ("sync", 0)
        if callable(fns):
            fns = [fns]
        pr = tuple(r for r in reads if r in self.psum_res)
        if pr:
            reads = tuple(r for r in reads if r not in self.psum_res)
            writes = tuple(writes) + pr
        self._waits(eng, self._need(reads, writes))
        self.semn[eng] += 1
        tok = self.semn[eng]
        h = self.semh[eng]
        for f in fns[:-1]:
            self.items[eng].append(f)
        self.items[eng].append(lambda e, f=fns[-1], h=h: f(e).then_inc(h, 1))
        self.nops += len(fns)
        self._update(reads, writes, (eng, tok))
        return (eng, tok)

    def dma(self, eng, dst, src, reads=(), writes=(), slow=False):
        if self.dead:
            return ("sync", 0)
        assert len(writes) == 1
        self._waits(eng, self._need(reads, writes))
        dname = self.dma_sem(eng, writes[0])
        self.semn[dname] += 16
        tok = self.semn[dname]
        h = self.semh[dname]
        self.items[eng].append(
            lambda e, o=dst, i=src, h=h, sl=slow: e.dma_start(
                out=o, in_=i, allow_slow_non_contiguous=sl).then_inc(h, 16))
        self.nops += 1
        self._update(reads, writes, (dname, tok))
        return (dname, tok)

    def final_wait(self, eng, reads):
        self._waits(eng, self._need(reads, ()))

    def barrier(self):
        if self.dead:
            return
        if not hasattr(self, "bsem"):
            self.bsem = self.st.enter_context(self.nc.semaphore("barrier"))
            self.bn = 0
        for sn, cnt in sorted(self.semn.items()):
            if sn == "sync" or cnt == 0:
                continue
            if self.waited["sync"].get(sn, 0) >= cnt:
                continue
            self.waited["sync"][sn] = cnt
            h = self.semh[sn]
            self.items["sync"].append(lambda e, h=h, v=cnt: e.wait_ge(h, v))
        self.bn += 1
        bh = self.bsem
        bv = self.bn
        self.items["sync"].append(lambda e, h=bh, v=bv: e.nop().then_inc(h, 1))
        self.semn["sync"] += 1
        snap = dict(self.semn)
        for eng in ENGS:
            if eng != "sync":
                self.items[eng].append(lambda e, h=bh, v=bv: e.wait_ge(h, v))
            self.waited[eng] = dict(snap)

    def realize(self):
        with self.nc.Block() as block:
            for eng in ENGS:
                items = self.items[eng]
                if items:
                    getattr(block, eng)(lambda e, it=items: [f(e) for f in it])



_G = {}
for _i in range(16):
    _G["xtm%d" % _i] = "xtmg"
    _G["wo_%d" % _i] = "wog"
for _sl in range(2):
    for _dc in range(8):
        _G["wb%d_%d" % (_sl, _dc)] = "wbg%d" % _sl
for _w in range(6):
    _G["wdq%d" % _w] = "wdqg"
for _nm in ("idb", "idf", "onesb", "onesf", "tri", "lnw", "lnb", "bcol",
            "la_f", "lb_f", "cdf", "cb", "nw", "nb", "dtbr", "alogr"):
    _G[_nm] = "constg"
for _nm in ("hidacum", "lodacum", "hidcombo", "lodcombo", "acumd", "combod"):
    _G[_nm] = "hilog"
for _nm in ("qst", "sst"):
    _G[_nm] = "dqg"
Prog.SEM_GROUP = _G


def build_program(tap=None):
    nc = bass.Bass("TRN2", target_bir_lowering=False)
    st = ExitStack()

    def din(name, shape, dt=F32):
        return nc.dram_tensor(name, shape, dt, kind="ExternalInput").ap()

    xin = din("xin", [D, T])
    win = din("win", [D, DPROJ])
    wqt = din("wqt", [6, D, D], I32)
    srt = din("srt", [6, D, D])
    biasc = din("biasc", [128, 48])
    lat = din("lat", [128, 6 * 8 * 32])
    lbt = din("lbt", [32, 6 * D])
    lnw3 = din("lnw3", [128, 24])
    lnb3 = din("lnb3", [128, 24])
    wout = din("wout", [DI, D])
    nwc = din("nwc", [128, 16])
    nbc = din("nbc", [128, 16])
    cdiag = din("cdiag", [128, 4 * 18 * 128])
    cbias = din("cbias", [128, 18])
    dtb = din("dtb", [32, 1])
    alog = din("alog", [32, 1])
    identb = din("identb", [128, 128], BF16)
    identf = din("identf", [128, 128])
    trimask = din("trimask", [128, 128])
    onesbf = din("onesbf", [128, 128], BF16)
    onesf = din("onesf", [128, 512])

    out_t = nc.dram_tensor("out", [D, T], F32, kind="ExternalOutput").ap()
    wdq = nc.dram_tensor("wdq", [6, D, D], BF16, kind="Internal").ap()
    winb_d = nc.dram_tensor("winb_d", [D, DPROJ], BF16, kind="Internal").ap()
    woutb_d = nc.dram_tensor("woutb_d", [DI, D], BF16, kind="Internal").ap()
    zsil_d = nc.dram_tensor("zsil_d", [T, DI], BF16, kind="Internal").ap()
    xbcpre_d = nc.dram_tensor("xbcpre_d", [DCONV, T], BF16, kind="Internal").ap()
    xtm_d = nc.dram_tensor("xtm_d", [T, DI], BF16, kind="Internal").ap()
    acum_d = nc.dram_tensor("acum_d", [32, T], F32, kind="Internal").ap()
    combo_d = nc.dram_tensor("combo_d", [32, T], F32, kind="Internal").ap()
    acumhi_d = nc.dram_tensor("acumhi_d", [32, T], F32, kind="Internal").ap()
    acumlo_d = nc.dram_tensor("acumlo_d", [32, T], F32, kind="Internal").ap()
    gend_d = nc.dram_tensor("gend_d", [NCH, 32], F32, kind="Internal").ap()

    p = Prog(nc, st)
    TAP = {"done": False}

    def sb(stk, name, shape, dt=F32):
        return stk.enter_context(nc.sbuf_tensor(name, shape, dt))

    def psum(stk, name, shape, dt=F32):
        return stk.enter_context(nc.psum_tensor(name, shape, dt))

    xb = sb(st, "xb", [128, 8 * T], BF16)
    hb = sb(st, "hb", [128, 8 * T], BF16)
    la_sb = sb(st, "la_sb", [128, 6 * 8 * 32], BF16)
    lb_sb = sb(st, "lb_sb", [32, 6 * D], BF16)
    mid_sb = sb(st, "mid_sb", [32, NT], BF16)
    bc_sb = sb(st, "bc_sb", [128, 48])
    lnw_sb = sb(st, "lnw_sb", [128, 24])
    lnb_sb = sb(st, "lnb_sb", [128, 24])
    idb_sb = sb(st, "idb_sb", [128, 128], BF16)
    idf_sb = sb(st, "idf_sb", [128, 128])
    onesb_sb = sb(st, "onesb_sb", [128, 128], BF16)
    onesf_sb = sb(st, "onesf_sb", [128, 512])
    tri_sb = sb(st, "tri_sb", [128, 128])
    xstage = sb(st, "xstage", [128, 2 * NT])

    def mmf(lhsT, rhs, out, start, stop):
        return lambda e, a=lhsT, b=rhs, o=out, s1=start, s2=stop: \
            e.matmul(out=o, lhsT=a, rhs=b, start=s1, stop=s2)

    def do_tap(src_ap, reads):
        TAP["done"] = True
        tp = nc.dram_tensor("tapo", list(src_ap.shape), src_ap.dtype,
                            kind="ExternalOutput").ap()
        p.dma("gpsimd", tp, src_ap, reads=reads, writes=("tapped",))
        p.dead = True

    with ExitStack() as st0:
        la_f = sb(st0, "la_f", [128, 6 * 8 * 32])
        lb_f = sb(st0, "lb_f", [32, 6 * D])
        for dst, src, rn in [
                (idb_sb[:], identb, "idb"), (idf_sb[:], identf, "idf"),
                (onesb_sb[:], onesbf, "onesb"), (onesf_sb[:], onesf, "onesf"),
                (tri_sb[:], trimask, "tri"), (lnw_sb[:], lnw3, "lnw"),
                (lnb_sb[:], lnb3, "lnb"), (bc_sb[:], biasc, "bcol"),
                (la_f[:], lat, "la_f"), (lb_f[:], lbt, "lb_f")]:
            p.dma("sync", dst, src, writes=(rn,))
        p.op("vector", lambda e: e.tensor_copy(out=la_sb[:], in_=la_f[:]),
             reads=("la_f",), writes=("la",))
        p.op("vector", lambda e: e.tensor_copy(out=lb_sb[:], in_=lb_f[:]),
             reads=("lb_f",), writes=("lb",))
        for k in range(16):
            sl = xstage[:, (k % 2) * NT:(k % 2 + 1) * NT]
            dc, Hh = k // 2, k % 2
            src = xin[dc * 128:(dc + 1) * 128, Hh * NT:(Hh + 1) * NT]
            dst = out_t[dc * 128:(dc + 1) * 128, Hh * NT:(Hh + 1) * NT]
            p.dma("sync", sl, src, writes=("xs%d" % (k % 2),))
            p.op("vector", lambda e, s=sl, d=xb[:, dc * T + Hh * NT: dc * T + Hh * NT + NT]:
                 e.tensor_copy(out=d, in_=s), reads=("xs%d" % (k % 2),), writes=("xb",))
            p.dma("gpsimd", dst, sl, reads=("xs%d" % (k % 2),), writes=("xd",))

    def emit_ffn(call):
        with ExitStack() as fst:
            gbuf = sb(fst, "gbuf%d" % call, [128, 8 * NT], BF16)
            yraw = sb(fst, "yraw%d" % call, [128, 8 * NT], BF16)
            wbuf = sb(fst, "wbuf%d" % call, [128, 2 * 8 * D], BF16)
            qi_sb = sb(fst, "qi%d" % call, [128, 2 * D], I32)
            sr_sb = sb(fst, "sr%d" % call, [128, 2 * D])
            qf_sb = sb(fst, "qf%d" % call, [128, D])
            ps_y = [psum(fst, "fy0_%d" % call, [128, NT]),
                    psum(fst, "fy1_%d" % call, [128, NT])]
            ps_s0 = psum(fst, "fs0_%d" % call, [128, NT])
            ps_s1 = psum(fst, "fs1_%d" % call, [128, NT])
            p.psum_res |= {"fy0", "fy1", "fs0", "fs1"}
            y2 = sb(fst, "y2_%d" % call, [128, NT], BF16)
            y2b = sb(fst, "y2b_%d" % call, [128, NT], BF16)
            y2s = [y2, y2b]
            r_rep = sb(fst, "rrep%d" % call, [128, NT], BF16)
            nmr_rep = sb(fst, "nmrep%d" % call, [128, NT], BF16)
            mu_rep = sb(fst, "murep%d" % call, [128, NT])
            m2_rep = sb(fst, "m2rep%d" % call, [128, NT])
            var_rep = sb(fst, "varep%d" % call, [128, NT])

            def stats_rows(ps_sum, rs_sum, ps_sq, rs_sq, nfeat):
                inv = 1.0 / nfeat
                p.op("vector", lambda e: e.tensor_scalar(
                    out=mu_rep[:], in0=ps_sum, scalar1=-inv, scalar2=None,
                    op0=OP.mult), reads=rs_sum, writes=("mu",))
                p.op("vector", lambda e: e.tensor_scalar(
                    out=var_rep[:], in0=ps_sq, scalar1=inv, scalar2=float(EPS),
                    op0=OP.mult, op1=OP.add), reads=rs_sq, writes=("var",))
                if tap == "sA" and not TAP["done"]:
                    do_tap(var_rep[:], ("var",))
                    return
                p.op("scalar", lambda e: e.activation(
                    out=m2_rep[:], in_=ps_sum, func=AF.Square, scale=inv),
                    reads=rs_sum, writes=("m2",))
                p.op("vector", lambda e: e.tensor_tensor(
                    out=var_rep[:], in0=var_rep[:], in1=m2_rep[:], op=OP.subtract),
                    reads=("var", "m2"), writes=("var",))
                if tap == "sB" and not TAP["done"]:
                    do_tap(var_rep[:], ("var",))
                    return
                p.op("scalar", lambda e: e.activation(
                    out=m2_rep[:], in_=var_rep[:], func=AF.Ln),
                    reads=("var",), writes=("m2",))
                p.op("scalar", lambda e: e.activation(
                    out=r_rep[:], in_=m2_rep[:], func=AF.Exp, scale=-0.5),
                    reads=("m2",), writes=("rr",))
                p.op("vector", lambda e: e.tensor_tensor(
                    out=nmr_rep[:], in0=mu_rep[:], in1=r_rep[:], op=OP.mult),
                    reads=("mu", "rr"), writes=("nmr",))
                if tap == "sC" and not TAP["done"]:
                    do_tap(nmr_rep[:], ("nmr",))
                    return

            def top_ln_to(dres):
                for H in range(2):
                    t0 = H * NT
                    for dc in range(8):
                        xs = xb[:, dc * T + t0: dc * T + t0 + NT]
                        p.op("tensor",
                             [mmf(onesb_sb[:], xs[:, 0:512], ps_s0[:, 0:512], dc == 0, dc == 7),
                              mmf(onesb_sb[:], xs[:, 512:NT], ps_s0[:, 512:NT], dc == 0, dc == 7)],
                             reads=("onesb", "xb"), writes=("fs0",))

                    def sq_mms(dc):
                        y2c = y2s[dc % 2]
                        p.op("tensor",
                             [mmf(onesb_sb[:], y2c[:, 0:512], ps_s1[:, 0:512], dc == 0, dc == 7),
                              mmf(onesb_sb[:], y2c[:, 512:NT], ps_s1[:, 512:NT], dc == 0, dc == 7)],
                             reads=("onesb", "y2%d" % (dc % 2)), writes=("fs1",))

                    for dc in range(8):
                        xs = xb[:, dc * T + t0: dc * T + t0 + NT]
                        p.op("scalar", lambda e, s=xs, d=y2s[dc % 2]: e.activation(
                            out=d[:], in_=s, func=AF.Square), reads=("xb",),
                            writes=("y2%d" % (dc % 2),))
                        if dc >= 1:
                            sq_mms(dc - 1)
                    sq_mms(7)
                    stats_rows(ps_s0[:], ("fs0",), ps_s1[:], ("fs1",), D)
                    for dc in range(8):
                        xs = xb[:, dc * T + t0: dc * T + t0 + NT]
                        hs = hb[:, dc * T + t0: dc * T + t0 + NT]
                        y2c = y2s[dc % 2]
                        yr2 = "y2%d" % (dc % 2)
                        p.op("vector", lambda e, a=xs, d=y2c: e.tensor_tensor(
                            out=d[:], in0=a, in1=r_rep[:], op=OP.mult),
                            reads=("xb", "rr"), writes=(yr2,))
                        p.op("vector", lambda e, d=hs, sc=y2c: e.tensor_tensor(
                            out=d, in0=sc[:], in1=nmr_rep[:], op=OP.add),
                            reads=(yr2, "nmr"), writes=(dres,))

            def dequant_weight(w):
                slot = w % 2
                for dc in range(8):
                    par = dc % 2
                    qs = qi_sb[:, par * D:(par + 1) * D]
                    srs = sr_sb[:, par * D:(par + 1) * D]
                    p.dma("sync", qs, wqt[w, dc * 128:(dc + 1) * 128, :],
                          writes=("qst%d" % par,))
                    p.dma("sync", srs, srt[w, dc * 128:(dc + 1) * 128, :],
                          writes=("sst%d" % par,))
                    p.op("scalar", lambda e, q=qs: e.activation(
                        out=qf_sb[:], in_=q, func=AF.Copy, bias=-1.0,
                        scale=1.0 / 7.5), reads=("qst%d" % par,), writes=("qf",))
                    dst = wbuf[:, slot * 8 * D + dc * D: slot * 8 * D + (dc + 1) * D]
                    p.op("vector", lambda e, d=dst, s=srs: e.tensor_tensor(
                        out=d, in0=qf_sb[:], in1=s, op=OP.mult),
                        reads=("qf", "sst%d" % par), writes=("wb%d_%d" % (slot, dc),))
                    p.dma("gpsimd", wdq[w, dc * 128:(dc + 1) * 128, :], dst,
                          reads=("wb%d_%d" % (slot, dc),), writes=("wdq%d" % w,))

            def stream_weight(w):
                slot = w % 2
                for dc in range(8):
                    dst = wbuf[:, slot * 8 * D + dc * D: slot * 8 * D + (dc + 1) * D]
                    p.dma("scalar", dst, wdq[w, dc * 128:(dc + 1) * 128, :],
                          reads=("wdq%d" % w,), writes=("wb%d_%d" % (slot, dc),))

            def lora_mid(widx, rhs_sl, rhs_res):
                ops = []
                for dc in range(8):
                    la = la_sb[:, (widx * 8 + dc) * 32:(widx * 8 + dc + 1) * 32]
                    ops.append(mmf(la, rhs_sl(dc)[:, 0:512],
                                   ps_s0[0:32, 0:512], dc == 0, dc == 7))
                    ops.append(mmf(la, rhs_sl(dc)[:, 512:NT],
                                   ps_s0[0:32, 512:NT], dc == 0, dc == 7))
                p.op("tensor", ops, reads=("la",) + rhs_res, writes=("fs0",))
                p.op("scalar", lambda e: e.copy(out=mid_sb[:], in_=ps_s0[0:32, :]),
                     reads=("fs0",), writes=("mid",))

            def qlora_mms(widx, rhs_sl, rhs_res, och, pst, pres):
                slot = widx % 2
                ops = []
                rds = {"mid"} | set(rhs_res)
                for dc in range(8):
                    lh = wbuf[:, slot * 8 * D + dc * D + och * 128:
                              slot * 8 * D + dc * D + och * 128 + 128]
                    rds.add("wb%d_%d" % (slot, dc))
                    ops.append(mmf(lh, rhs_sl(dc)[:, 0:512], pst[:, 0:512], dc == 0, False))
                    ops.append(mmf(lh, rhs_sl(dc)[:, 512:NT], pst[:, 512:NT], dc == 0, False))
                lbs = lb_sb[:, widx * D + och * 128: widx * D + och * 128 + 128]
                ops.append(mmf(lbs, mid_sb[:, 0:512], pst[:, 0:512], False, True))
                ops.append(mmf(lbs, mid_sb[:, 512:NT], pst[:, 512:NT], False, True))
                p.op("tensor", ops, reads=tuple(rds) + ("lb",), writes=(pres,))

            top_ln_to("h")
            if tap == "h0_%d" % call:
                do_tap(hb[:], ("h",))
                return
            (dequant_weight if call == 0 else stream_weight)(0)
            (dequant_weight if call == 0 else stream_weight)(1)
            if tap == "wb_%d" % call:
                do_tap(wbuf[:], tuple("wb%d_%d" % (sl_, dc_) for sl_ in range(2)
                                      for dc_ in range(8)))
                return
            for blk in range(3):
                i0, i1 = 2 * blk, 2 * blk + 1
                for H in range(2):
                    t0 = H * NT
                    hsl = lambda dc, t0=t0: hb[:, dc * T + t0: dc * T + t0 + NT]
                    gsl = lambda dc: gbuf[:, dc * NT:(dc + 1) * NT]
                    gres = tuple("g%d" % i for i in range(8))
                    lora_mid(i0, hsl, ("h",))
                    if tap == "mid_%d_%d_%d" % (call, blk, H):
                        do_tap(mid_sb[:], ("mid",))
                        return

                    def emit_stats_mms(och):
                        ysl = yraw[:, och * NT:(och + 1) * NT]
                        y2c = y2s[och % 2]
                        p.op("tensor",
                             [mmf(onesb_sb[:], ysl[:, 0:512], ps_s0[:, 0:512], och == 0, och == 7),
                              mmf(onesb_sb[:], ysl[:, 512:NT], ps_s0[:, 512:NT], och == 0, och == 7)],
                             reads=("onesb", "yr%d" % och), writes=("fs0",))
                        p.op("tensor",
                             [mmf(onesb_sb[:], y2c[:, 0:512], ps_s1[:, 0:512], och == 0, och == 7),
                              mmf(onesb_sb[:], y2c[:, 512:NT], ps_s1[:, 512:NT], och == 0, och == 7)],
                             reads=("onesb", "y2%d" % (och % 2)), writes=("fs1",))

                    for och in range(8):
                        pr = "fy%d" % (och % 2)
                        qlora_mms(i0, hsl, ("h",), och, ps_y[och % 2], pr)
                        bcol = bc_sb[:, i0 * 8 + och: i0 * 8 + och + 1]
                        ysl = yraw[:, och * NT:(och + 1) * NT]
                        p.op("scalar", lambda e, d=ysl, s=ps_y[och % 2], b=bcol:
                             e.activation(out=d, in_=s[:], func=AF.Identity,
                                          bias=b, scale=1.0),
                             reads=(pr, "bcol"), writes=("yr%d" % och,))
                        p.op("scalar", lambda e, s=ps_y[och % 2], b=bcol,
                             d=y2s[och % 2]:
                             e.activation(out=d[:], in_=s[:], func=AF.Square,
                                          bias=b, scale=1.0),
                             reads=(pr, "bcol"), writes=("y2%d" % (och % 2),))
                        if och >= 1:
                            emit_stats_mms(och - 1)
                    emit_stats_mms(7)
                    if tap == "yraw_%d_%d_%d" % (call, blk, H):
                        do_tap(yraw[:], tuple("yr%d" % i for i in range(8)))
                        return
                    stats_rows(ps_s0[:], ("fs0",), ps_s1[:], ("fs1",), D)
                    if tap == "pre" and not TAP["done"]:
                        do_tap(y2[:], ("y2",))
                        return
                    for och in range(8):
                        ysl = yraw[:, och * NT:(och + 1) * NT]
                        p.op("vector", lambda e, a=ysl: e.tensor_tensor(
                            out=y2[:], in0=a, in1=r_rep[:], op=OP.mult),
                            reads=("yr%d" % och, "rr"), writes=("y2",))
                        if tap == "ap%d" % och and not TAP["done"]:
                            do_tap(y2[:], ("y2",))
                            return
                        p.op("vector", lambda e: e.tensor_tensor(
                            out=y2[:], in0=y2[:], in1=nmr_rep[:], op=OP.add),
                            reads=("y2", "nmr"), writes=("y2",))
                        if tap == "aq%d" % och and not TAP["done"]:
                            do_tap(y2[:], ("y2",))
                            return
                        gcol = blk * 8 + och
                        p.op("scalar", lambda e, d=gsl(och), gc=gcol: e.activation(
                            out=d, in_=y2[:], func=AF.Gelu,
                            bias=lnb_sb[:, gc:gc + 1], scale=lnw_sb[:, gc:gc + 1]),
                            reads=("y2", "lnw", "lnb"), writes=("g%d" % och,))
                        if tap == "ag%d" % och and not TAP["done"]:
                            do_tap(gbuf[:], ("g%d" % och,))
                            return
                    if tap == "g_%d_%d_%d" % (call, blk, H):
                        do_tap(gbuf[:], gres)
                        return
                    lora_mid(i1, gsl, gres)
                    for och in range(8):
                        pr = "fy%d" % (och % 2)
                        qlora_mms(i1, gsl, gres, och, ps_y[och % 2], pr)
                        bcol = bc_sb[:, i1 * 8 + och: i1 * 8 + och + 1]
                        hs = hsl(och)
                        p.op("vector", lambda e, d=hs, s=ps_y[och % 2], b=bcol:
                             e.scalar_tensor_tensor(out=d, in0=s[:], scalar=b,
                                                    in1=d, op0=OP.add, op1=OP.add),
                             reads=(pr, "bcol", "h"), writes=("h",))
                if tap == "hblk_%d_%d" % (call, blk):
                    do_tap(hb[:], ("h",))
                    return
                for wnext in (i0 + 2, i0 + 3):
                    if wnext <= 5:
                        (dequant_weight if call == 0 else stream_weight)(wnext)
            for k in range(16):
                dc, H = k // 2, k % 2
                sl = xstage[:, (k % 2) * NT:(k % 2 + 1) * NT]
                xres = "xs%d" % (k % 2)
                dsl = out_t[dc * 128:(dc + 1) * 128, H * NT:(H + 1) * NT]
                hs = hb[:, dc * T + H * NT: dc * T + H * NT + NT]
                p.dma("sync", sl, dsl, reads=("xd",), writes=(xres,))
                p.op("vector", lambda e, x=sl, h=hs: e.scalar_tensor_tensor(
                    out=x, in0=h, scalar=0.5, in1=x, op0=OP.mult, op1=OP.add),
                    reads=("h", xres), writes=(xres,))
                p.op("vector", lambda e, x=sl, d=xb[:, dc * T + H * NT: dc * T + H * NT + NT]:
                     e.tensor_copy(out=d, in_=x), reads=(xres,), writes=("xb",))
                p.dma("gpsimd", dsl, sl, reads=(xres,), writes=("xd",))
            if call == 0:
                top_ln_to("h")
                if tap == "u":
                    do_tap(hb[:], ("h",))

    def emit_mixer():
        with ExitStack() as mst:
            gbuf = sb(mst, "mgb", [128, 8 * NT], BF16)
            yraw = sb(mst, "myr", [128, T], BF16)
            P1 = psum(mst, "P1", [128, 2048])
            P2 = psum(mst, "P2", [128, 1024])
            P3 = psum(mst, "P3", [128, 512])
            PT = psum(mst, "PT", [128, 1024], BF16)
            p.psum_res |= {"P1q0", "P1q1", "P1q2", "P1q3", "P2", "P3", "PT"}
            P1r = ("P1q0", "P1q1", "P1q2", "P1q3")

            acum = sb(mst, "acum", [32, T])
            combo = sb(mst, "combo", [32, T])
            acumT = sb(mst, "acumT", [128, NCH * 32])
            comboT = None
            vT = sb(mst, "vT", [128, NCH * 32])
            comboT = sb(mst, "comboT", [128, NCH * 32])
            expAT = sb(mst, "expAT", [128, NCH * 32], BF16)
            gamrep = sb(mst, "gamrep", [128, 32])
            ends32 = sb(mst, "ends32", [32, NCH])
            endshi = sb(mst, "endshi", [32, NCH], BF16)
            endslo = sb(mst, "endslo", [32, NCH], BF16)
            bfm = sb(mst, "bfm", [128, T], BF16)
            cfm = sb(mst, "cfm", [128, T], BF16)
            btm = sb(mst, "btm", [128, NCH * 128], BF16)
            rst = sb(mst, "rst", [128, DI])
            rstb = sb(mst, "rstb", [128, DI], BF16)
            xs_sb = sb(mst, "xs_sb", [128, DI], BF16)
            gm_sb = sb(mst, "gm_sb", [128, 128], BF16)
            ygt = sb(mst, "ygt", [128, DI], BF16)
            ccol = sb(mst, "ccol", [128, 8])
            sumc = sb(mst, "sumc", [128, 1])
            sqc = sb(mst, "sqc", [128, 1])
            mucol = sb(mst, "mucol", [128, 1])
            m2col = sb(mst, "m2col", [128, 1])
            varcol = sb(mst, "varcol", [128, 1])
            rcol = sb(mst, "rcol", [128, 1])
            nmrcol = sb(mst, "nmrcol", [128, 1])
            wio = sb(mst, "wio", [128, 2048])
            grow = wio[32:33, 0:NCH * 32]
            wiob = wio[:].bitcast(BF16)
            PTf32 = PT[:].bitcast(F32)
            wob2a = acumT[:].bitcast(BF16)
            wob2b = comboT[:].bitcast(BF16)
            esb = gbuf[:, 4096:4096 + NT].bitcast(BF16)[:, 0:NT] \
                if False else gbuf[:, 4096:4096 + NT]
            esb2 = gbuf[:, 5120:5120 + NT]
            mn8 = gbuf[:, 6144:6144 + NT]
            gdthi = gbuf[0:32, 7168:7200]
            gdtlo = gbuf[0:32, 7200:7232]
            wzb = sb(mst, "wzb", [128, 8 * 512], BF16)
            wxb = sb(mst, "wxb", [128, 2 * 8 * 128], BF16)
            wdtb = sb(mst, "wdtb", [128, 8 * 32], BF16)
            wob = sb(mst, "wob", [128, 16 * 128], BF16)
            prebuf = sb(mst, "prebuf", [128, T + 3], BF16)
            aThi = prebuf[:, 0:NCH * 32]
            aTlo = prebuf[:, NCH * 32:2 * NCH * 32]
            cThi = prebuf[:, 2 * NCH * 32:3 * NCH * 32]
            cTlo = prebuf[:, 3 * NCH * 32:4 * NCH * 32]
            cdg = sb(mst, "cdg", [128, 2 * 4 * 128], BF16)
            cb_sb = sb(mst, "cb_sb", [128, 18])
            nw_sb = sb(mst, "nw_sb", [128, 16])
            nb_sb = sb(mst, "nb_sb", [128, 16])
            nbb_sb = sb(mst, "nbb_sb", [128, 16], BF16)
            dtb_sb = sb(mst, "dtb_sb", [32, 1])
            alog_sb = sb(mst, "alog_sb", [32, 1])
            acol = sb(mst, "acol", [32, 1])

            for dst, src, rn in [(cb_sb[:], cbias, "cb"), (nw_sb[:], nwc, "nw"),
                                 (nb_sb[:], nbc, "nb"), (dtb_sb[:], dtb, "dtbr"),
                                 (alog_sb[:], alog, "alogr")]:
                p.dma("sync", dst, src, writes=(rn,))

            for dc in range(8):
                for cg in range(5):
                    c0 = cg * 1024
                    cw = min(1024, DPROJ - c0)
                    p.dma("sync", wio[:, 0:cw], win[dc * 128:(dc + 1) * 128, c0:c0 + cw],
                          writes=("wio",))
                    p.op("vector", lambda e, w=cw: e.tensor_copy(
                        out=gbuf[:, 0:w], in_=wio[:, 0:w]),
                        reads=("wio",), writes=("gsc",))
                    p.dma("gpsimd", winb_d[dc * 128:(dc + 1) * 128, c0:c0 + cw],
                          gbuf[:, 0:cw], reads=("gsc",), writes=("winb",))
            for dc in range(16):
                p.dma("sync", wio[:, 0:D], wout[dc * 128:(dc + 1) * 128, :],
                      writes=("wio",))
                p.op("vector", lambda e, c=nw_sb[:, dc:dc + 1]: e.tensor_scalar(
                    out=gbuf[:, 0:D], in0=wio[:, 0:D], scalar1=c, scalar2=None,
                    op0=OP.mult), reads=("wio", "nw"), writes=("gsc",))
                p.dma("gpsimd", woutb_d[dc * 128:(dc + 1) * 128, :], gbuf[:, 0:D],
                      reads=("gsc",), writes=("woutbd",))

            p.dma("scalar", wdtb[:].rearrange("p (g n) -> p g n", n=32),
                  winb_d.rearrange("(g p) n -> p g n", p=128)[:, :, 4352:4384],
                  reads=("winb",), writes=("wdtb",))
            if True:
                gf32 = gbuf[:].bitcast(F32)
                dte = gf32[0:32, 0:T]
                lndt = gf32[0:32, T:2 * T]
                for half in range(2):
                    ops = []
                    for dc in range(8):
                        for s4 in range(2):
                            ops.append(mmf(
                                wdtb[:, dc * 32:(dc + 1) * 32],
                                hb[:, dc * T + half * NT + s4 * 512:
                                   dc * T + half * NT + s4 * 512 + 512],
                                P2[0:32, s4 * 512:(s4 + 1) * 512], dc == 0, dc == 7))
                    p.op("tensor", ops, reads=("wdtb", "h"), writes=("P2",))
                    p.op("scalar", lambda e, h=half: e.copy(
                        out=acum[:, h * NT:(h + 1) * NT], in_=P2[0:32, :]),
                        reads=("P2",), writes=("acum",))
                p.op("scalar", lambda e: e.activation(
                    out=acol[:], in_=alog_sb[:], func=AF.Exp),
                    reads=("alogr",), writes=("acol",))
                p.op("vector", lambda e: e.tensor_scalar(
                    out=acol[:], in0=acol[:], scalar1=-1.0, scalar2=None, op0=OP.mult),
                    reads=("acol",), writes=("acol",))
                p.op("scalar", lambda e: e.activation(
                    out=dte[:], in_=acum[:], func=AF.Exp, bias=dtb_sb[:], scale=1.0),
                    reads=("acum", "dtbr", "gsc"), writes=("gsc",))
                p.op("scalar", lambda e: e.activation(
                    out=dte[:], in_=dte[:], func=AF.Ln, bias=1.0, scale=1.0),
                    reads=("gsc",), writes=("gsc",))
                p.op("scalar", lambda e: e.activation(
                    out=lndt[:], in_=dte[:], func=AF.Ln),
                    reads=("gsc",), writes=("gsc",))
                p.op("vector", lambda e: e.tensor_scalar(
                    out=combo[:], in0=dte[:], scalar1=acol[:], scalar2=None,
                    op0=OP.mult), reads=("gsc", "acol"), writes=("combo",))
                for c in range(NCH):
                    p.op("vector", lambda e, c=c: e.tensor_tensor_scan(
                        out=acum[:, c * CS:(c + 1) * CS],
                        data0=onesf_sb[0:32, 0:CS],
                        data1=combo[:, c * CS:(c + 1) * CS], initial=0.0,
                        op0=OP.mult, op1=OP.add),
                        reads=("combo", "onesf"), writes=("acum",))
                p.op("vector", lambda e: e.tensor_tensor(
                    out=combo[:], in0=lndt[:], in1=acum[:], op=OP.subtract),
                    reads=("gsc", "acum"), writes=("combo",))
                for c in range(NCH):
                    p.op("vector", lambda e, c=c: e.tensor_scalar(
                        out=lndt[:, c * CS:(c + 1) * CS],
                        in0=combo[:, c * CS:(c + 1) * CS],
                        scalar1=acum[:, c * CS + CS - 1: c * CS + CS],
                        scalar2=None, op0=OP.add),
                        reads=("combo", "acum", "gsc"), writes=("gsc",))
                p.op("scalar", lambda e: e.activation(
                    out=lndt[:], in_=lndt[:], func=AF.Exp),
                    reads=("gsc",), writes=("gsc",))
                ttmp = wio[64:96, 0:2048]
                for nmres, src, sres, dst in [("aT", acum, "acum", acumT),
                                              ("cT", combo, "combo", comboT),
                                              ("vTT", lndt, "gsc", vT)]:
                    p.op("vector", lambda e, srcl=src: e.transpose(
                        out=ttmp, in_=srcl[:]), reads=(sres, "wio"),
                        writes=("wio",))
                    for w in range(4):
                        p.dma("sync",
                              dst[w * 32:(w + 1) * 32, :]
                              .rearrange("p (c h) -> p c h", h=32),
                              ttmp.rearrange("p (c q) -> p c q", q=128)
                              [:, :, w * 32:w * 32 + 32],
                              reads=("wio",), writes=(nmres,))
                if tap == "aTT":
                    do_tap(acumT[:], ("aT",))
                p.op("scalar", lambda e: e.activation(
                    out=expAT[:], in_=acumT[:], func=AF.Exp),
                    reads=("aT",), writes=("expAT",))
            p.op("vector", lambda e: e.tensor_copy(
                out=ends32[:].unsqueeze(2),
                in_=acum[:].rearrange("p (c q) -> p c q", q=CS)[:, :, CS - 1:CS]),
                reads=("acum",), writes=("ends",))
            p.op("vector", lambda e: e.tensor_copy(out=endshi[:], in_=ends32[:]),
                 reads=("ends",), writes=("ehi",))
            p.op("vector", lambda e: e.tensor_copy(
                out=wio[0:32, 512:512 + NCH], in_=endshi[:]),
                reads=("ehi",), writes=("wio",))
            p.op("vector", lambda e: e.tensor_tensor(
                out=endslo[:], in0=ends32[:], in1=wio[0:32, 512:512 + NCH],
                op=OP.subtract), reads=("ends", "wio"), writes=("elo",))
            p.op("vector", lambda e: e.tensor_scalar(
                out=mn8[:].rearrange("p (j l) -> p j l", l=CS),
                in0=tri_sb[:].unsqueeze(1).to_broadcast([128, 8, CS]),
                scalar1=1.0, scalar2=30000.0, op0=OP.subtract, op1=OP.mult),
                reads=("tri", "gsc"), writes=("mn8",))
            if tap == "dt":
                do_tap(acum[:], ("acum",))
                return

            winb_r = winb_d.rearrange("(g p) n -> p g n", p=128)
            for zg in range(4):
                p.dma("scalar", wzb[:].rearrange("p (g n) -> p g n", n=512),
                      winb_r[:, :, zg * 512:(zg + 1) * 512],
                      reads=("winb",), writes=("wzb",))
                for tc in range(16):
                    q = tc % 4
                    ops = []
                    for dc in range(8):
                        lh = hb[:, dc * T + tc * 128: dc * T + tc * 128 + 128]
                        ops.append(mmf(lh, wzb[:, dc * 512:(dc + 1) * 512],
                                       P1[:, q * 512:(q + 1) * 512],
                                       dc == 0, dc == 7))
                    p.op("tensor", ops, reads=("wzb", "h"), writes=(P1r[q],))
                    p.op("scalar", lambda e, q=q: e.activation(
                        out=gbuf[:, q * 512:(q + 1) * 512],
                        in_=P1[:, q * 512:(q + 1) * 512],
                        func=AF.Silu), reads=(P1r[q],), writes=("gsc",))
                    p.dma("gpsimd",
                          zsil_d[tc * 128:(tc + 1) * 128, zg * 512:(zg + 1) * 512],
                          gbuf[:, q * 512:(q + 1) * 512],
                          reads=("gsc",), writes=("zsil",))

            for och in range(18):
                slot = och % 2
                p.dma("scalar",
                      wxb[:, slot * 1024: slot * 1024 + 1024]
                      .rearrange("p (g n) -> p g n", n=128),
                      winb_r[:, :, 2048 + och * 128: 2048 + (och + 1) * 128],
                      reads=("winb",), writes=("wxb%d" % slot,))
                ops = []
                for dc in range(8):
                    lh = wxb[:, slot * 1024 + dc * 128: slot * 1024 + (dc + 1) * 128]
                    for s4 in range(4):
                        ops.append(mmf(lh, hb[:, dc * T + s4 * 512: dc * T + (s4 + 1) * 512],
                                       P1[:, s4 * 512:(s4 + 1) * 512], dc == 0, dc == 7))
                p.op("tensor", ops, reads=("wxb%d" % slot, "h"), writes=P1r)
                p.op("vector", lambda e: e.tensor_copy(out=yraw[:, 0:T], in_=P1[:]),
                     reads=P1r, writes=("ysc",))
                p.dma("gpsimd", xbcpre_d[och * 128:(och + 1) * 128, :], yraw[:, 0:T],
                      reads=("ysc",), writes=("xbcpre",))
            if tap == "inproj":
                do_tap(hb[:], ("h",))
                return

            for och in range(18):
                p.dma("sync", prebuf[:, 3:3 + T],
                      xbcpre_d[och * 128:(och + 1) * 128, :],
                      reads=("xbcpre",), writes=("prebuf",))
                p.op("vector", lambda e: e.memset(ap=prebuf[:, 0:3], constant=0.0),
                     writes=("prebuf",))
                slot = och % 2
                p.dma("scalar", wio[:, 0:512].rearrange("p (i q) -> p i q", q=128),
                      cdiag.rearrange("p (i c q) -> p c i q", c=18, q=128)[:, och, :, :],
                      reads=(), writes=("wio",))
                p.op("vector", lambda e, s=slot: e.tensor_copy(
                    out=cdg[:, s * 512:(s + 1) * 512], in_=wio[:, 0:512]),
                    reads=("wio",), writes=("cdg%d" % slot,))
                ops = []
                for s4 in range(4):
                    for i in range(4):
                        ops.append(mmf(cdg[:, slot * 512 + i * 128: slot * 512 + (i + 1) * 128],
                                       prebuf[:, s4 * 512 + i: s4 * 512 + i + 512],
                                       P1[:, s4 * 512:(s4 + 1) * 512], i == 0, i == 3))
                p.op("tensor", ops, reads=("cdg%d" % slot, "prebuf"), writes=P1r)
                dsttile = bfm if och == 16 else (cfm if och == 17 else prebuf[:, 3:3 + T])
                dres = "bfm" if och == 16 else ("cfm" if och == 17 else "prebuf")
                p.op("scalar", lambda e, d=dsttile, c=cb_sb[:, och:och + 1]:
                     e.activation(out=d[:], in_=P1[:], func=AF.Silu, bias=c, scale=1.0),
                     reads=P1r + ("cb",), writes=(dres,))
                if och < 17:
                    for grp in range(2):
                        ops = []
                        for k in range(8):
                            tc = grp * 8 + k
                            ops.append(lambda e, s=dsttile[:, tc * 128:(tc + 1) * 128],
                                       o=PT[:, k * 128:(k + 1) * 128]:
                                       e.transpose(out=o, in_=s, identity=idb_sb[:]))
                        p.op("tensor", ops, reads=(dres, "idb"), writes=("PT",))
                        if och < 16:
                            p.op("vector", lambda e: e.tensor_copy(
                                out=yraw[:, 0:1024], in_=PT[:]),
                                reads=("PT",), writes=("ysc",))
                            for k in range(8):
                                tc = grp * 8 + k
                                p.dma("gpsimd",
                                      xtm_d[tc * 128:(tc + 1) * 128,
                                            och * 128:(och + 1) * 128],
                                      yraw[:, k * 128:(k + 1) * 128],
                                      reads=("ysc",), writes=("xtm%d" % tc,))
                        else:
                            p.op("vector",
                                 lambda e, d=btm[:, grp * 1024:(grp + 1) * 1024]:
                                 e.tensor_copy(out=d, in_=PT[:]),
                                 reads=("PT",), writes=("btm",))
            if tap == "conv":
                do_tap(cfm[:], ("cfm",))
                return


            p.op("vector", lambda e: e.tensor_copy(out=nbb_sb[:], in_=nb_sb[:]),
                 reads=("nb",), writes=("nbb",))
            for dcg in range(16):
                slot = dcg % 2
                p.dma("scalar", wzb[:, slot * D: slot * D + D],
                      woutb_d[dcg * 128:(dcg + 1) * 128, :],
                      reads=("woutbd",), writes=("wzb",))
                ops = [mmf(wzb[:, slot * D + och * 128: slot * D + och * 128 + 128],
                           nbb_sb[:, dcg:dcg + 1], P3[:, och:och + 1],
                           dcg == 0 and och == 0, dcg == 15 and och == 7)
                       for och in range(8)]
                p.op("tensor", ops, reads=("wzb", "nbb"), writes=("P3",))
            p.op("vector", lambda e: e.tensor_copy(out=ccol[:], in_=P3[:, 0:8]),
                 reads=("P3",), writes=("ccol",))

            first = ["cfm"]
            for src, sres, dhi, dlo, rhi, rlo in [
                    (acumT, "aT", aThi, aTlo, "aThi", "aTlo"),
                    (comboT, "cT", cThi, cTlo, "cThi", "cTlo")]:
                p.op("vector", lambda e, s=src, d=dhi: e.tensor_copy(
                    out=d, in_=s[:]), reads=(sres,) + tuple(first),
                    writes=(rhi,))
                first = []
                p.op("vector", lambda e, d=dhi: e.tensor_copy(
                    out=wio[:, 0:NCH * 32], in_=d),
                    reads=(rhi,), writes=("wio",))
                p.op("vector", lambda e, s=src, d=dlo: e.tensor_tensor(
                    out=d, in0=s[:], in1=wio[:, 0:NCH * 32],
                    op=OP.subtract), reads=(sres, "wio"), writes=(rlo,))

            if tap == "tg":
                do_tap(acumT[:], ("wio",))
            with ExitStack() as sst:
                xtmc = sb(sst, "xtmc", [128, DI], BF16)
                zslc = sb(sst, "zslc", [128, DI], BF16)
                p.op("vector", lambda e: e.memset(ap=rst[:], constant=0.0),
                     writes=("rst",))
                p.op("vector", lambda e: e.memset(ap=rstb[:], constant=0.0),
                     writes=("rstb",))
                for c in range(NCH):
                    cl = slice(c * CS, (c + 1) * CS)
                    sl = 0
                    xtms = xtmc[:]
                    p.dma("sync", xtms, xtm_d[cl, :],
                          reads=("xtm%d" % c,), writes=("xtmc%d" % sl,))
                    p.dma("sync", zslc[:], zsil_d[cl, :],
                          reads=("zsil",), writes=("zslc%d" % sl,))
                    for s2 in range(2):
                        ops = [mmf(cfm[:, cl],
                                   rstb[:, s2 * 1024 + s4 * 512: s2 * 1024 + (s4 + 1) * 512],
                                   P2[:, s4 * 512:(s4 + 1) * 512], True, True)
                               for s4 in range(2)]
                        p.op("tensor", ops, reads=("cfm", "rstb"), writes=("P2",))
                        p.op("vector", lambda e, c=c, s2=s2: e.tensor_tensor(
                            out=xs_sb[:, s2 * 1024:(s2 + 1) * 1024]
                            .rearrange("p (h q) -> p h q", q=64),
                            in0=P2[:].rearrange("p (h q) -> p h q", q=64),
                            in1=expAT[:, c * 32 + s2 * 16: c * 32 + (s2 + 1) * 16]
                            .unsqueeze(2).to_broadcast([128, 16, 64]), op=OP.mult),
                            reads=("P2", "expAT"), writes=("xs",))
                    if tap == "tyo%d" % c:
                        do_tap(xs_sb[:], ("xs",))
                    p.op("tensor", mmf(bfm[:, cl], cfm[:, cl], P3[:, 0:128], True, True),
                         reads=("bfm", "cfm"), writes=("P3",))
                    p.op("vector", lambda e: e.tensor_tensor(
                        out=gm_sb[:], in0=P3[:, 0:128], in1=tri_sb[:], op=OP.mult),
                        reads=("P3", "tri"), writes=("gm",))
                    if tap == "tGm%d" % c:
                        do_tap(gm_sb[:], ("gm",))
                    for hf in range(4):
                        par = hf % 2
                        slot = wiob[:, par * 2048: par * 2048 + 2048]
                        sres = "wio" if par == 0 else "wioB"
                        esbX = esb if par == 0 else esb2
                        eres = "gse" if par == 0 else "gse2"
                        a8 = slice(c * 32 + hf * 8, c * 32 + hf * 8 + 8)
                        p.op("vector", lambda e, s=slot, a=a8: e.tensor_tensor(
                            out=s[:, 0:1024].rearrange("p (j l) -> p j l", l=CS),
                            in0=idb_sb[:].unsqueeze(1).to_broadcast([128, 8, CS]),
                            in1=aThi[:, a].unsqueeze(2).to_broadcast([128, 8, CS]),
                            op=OP.mult), reads=("idb", "aThi"), writes=(sres,))
                        p.op("vector", lambda e, s=slot, a=a8: e.tensor_tensor(
                            out=s[:, 1024:2048].rearrange("p (j l) -> p j l", l=CS),
                            in0=idb_sb[:].unsqueeze(1).to_broadcast([128, 8, CS]),
                            in1=aTlo[:, a].unsqueeze(2).to_broadcast([128, 8, CS]),
                            op=OP.mult), reads=("idb", "aTlo"), writes=(sres,))
                        for half in range(2):
                            bank = P3[:, 0:512] if half == 0 else PTf32[:, 0:512]
                            bres = "P3" if half == 0 else "PT"
                            c4 = slice(c * 32 + hf * 8 + half * 4,
                                       c * 32 + hf * 8 + half * 4 + 4)
                            h0 = half * 512
                            ops = [
                                mmf(onesb_sb[:], slot[:, h0:h0 + 512],
                                    bank, True, False),
                                mmf(onesb_sb[:], slot[:, 1024 + h0:1024 + h0 + 512],
                                    bank, False, False),
                                mmf(idb_sb[:], cThi[:, c4].unsqueeze(2)
                                    .to_broadcast([128, 4, CS]), bank, False, False),
                                mmf(idb_sb[:], cTlo[:, c4].unsqueeze(2)
                                    .to_broadcast([128, 4, CS]), bank, False, False),
                                mmf(idb_sb[:], mn8[:, h0:h0 + 512],
                                    bank, False, True)]
                            p.op("tensor", ops,
                                 reads=(sres, "cThi", "cTlo", "mn8", "onesb", "idb"),
                                 writes=(bres,))
                            p.op("scalar", lambda e, b=bank,
                                 o=esbX[:, h0:h0 + 512]:
                                 e.activation(out=o, in_=b, func=AF.Exp),
                                 reads=(bres,), writes=(eres,))
                        if tap == "tE%d_%d" % (c, hf):
                            do_tap(esbX[:], (eres,))
                        p.op("vector", lambda e, s=esbX: e.tensor_tensor(
                            out=s[:].rearrange("p (h l) -> p h l", l=128),
                            in0=s[:].rearrange("p (h l) -> p h l", l=128),
                            in1=gm_sb[:].unsqueeze(1).to_broadcast([128, 8, 128]),
                            op=OP.mult), reads=(eres, "gm"), writes=(eres,))
                        ops = []
                        for k in range(8):
                            h = hf * 8 + k
                            ops.append(mmf(esbX[:, k * 128:(k + 1) * 128],
                                           xtms[:, h * 64:(h + 1) * 64],
                                           P1[:, h * 64:(h + 1) * 64],
                                           k == 0, k == 7))
                        p.op("tensor", ops, reads=(eres, "xtmc%d" % sl),
                             writes=(P1r[hf],))
                    p.op("vector", lambda e: e.tensor_tensor(
                        out=ygt[:], in0=P1[:], in1=xs_sb[:], op=OP.add),
                        reads=P1r + ("xs",), writes=("ygt",))
                    p.op("vector", lambda e, z=zslc[:]: e.tensor_tensor(
                        out=ygt[:], in0=ygt[:], in1=z, op=OP.mult),
                        reads=("ygt", "zslc%d" % sl), writes=("ygt",))
                    if tap == "yg%d" % c:
                        do_tap(ygt[:], ("ygt",))
                    if tap == "yo%d" % c:
                        do_tap(xs_sb[:], ("xs",))
                    p.op("vector", lambda e: e.tensor_scalar(
                        out=yraw[:, 0:DI], in0=ygt[:], scalar1=1.0, scalar2=None,
                        op0=OP.mult, op1=OP.add, accum_out=sumc[:]),
                        reads=("ygt",), writes=("sumc", "ysc"))
                    p.op("scalar", lambda e: e.activation(
                        out=yraw[:, 0:DI], in_=ygt[:], func=AF.Square),
                        reads=("ygt",), writes=("ysc",))
                    p.op("vector", lambda e: e.tensor_scalar(
                        out=yraw[:, 0:DI], in0=yraw[:, 0:DI], scalar1=1.0,
                        scalar2=None, op0=OP.mult, op1=OP.add, accum_out=sqc[:]),
                        reads=("ysc",), writes=("sqc", "ysc"))
                    p.op("vector", lambda e: e.tensor_scalar(
                        out=mucol[:], in0=sumc[:], scalar1=-1.0 / DI, scalar2=None,
                        op0=OP.mult), reads=("sumc",), writes=("mucol",))
                    p.op("scalar", lambda e: e.activation(
                        out=m2col[:], in_=sumc[:], func=AF.Square, scale=1.0 / DI),
                        reads=("sumc",), writes=("m2col",))
                    p.op("vector", lambda e: e.tensor_scalar(
                        out=varcol[:], in0=sqc[:], scalar1=1.0 / DI, scalar2=float(EPS),
                        op0=OP.mult, op1=OP.add), reads=("sqc",), writes=("varcol",))
                    p.op("vector", lambda e: e.tensor_tensor(
                        out=varcol[:], in0=varcol[:], in1=m2col[:], op=OP.subtract),
                        reads=("varcol", "m2col"), writes=("varcol",))
                    p.op("scalar", lambda e: e.activation(
                        out=varcol[:], in_=varcol[:], func=AF.Ln),
                        reads=("varcol",), writes=("varcol",))
                    p.op("scalar", lambda e: e.activation(
                        out=rcol[:], in_=varcol[:], func=AF.Exp, scale=-0.5),
                        reads=("varcol",), writes=("rcol",))
                    p.op("vector", lambda e: e.tensor_tensor(
                        out=nmrcol[:], in0=mucol[:], in1=rcol[:], op=OP.mult),
                        reads=("mucol", "rcol"), writes=("nmrcol",))
                    p.op("vector", lambda e: e.tensor_scalar(
                        out=ygt[:], in0=ygt[:], scalar1=rcol[:], scalar2=nmrcol[:],
                        op0=OP.mult, op1=OP.add),
                        reads=("ygt", "rcol", "nmrcol"), writes=("ygt",))
                    if tap == "yln%d" % c:
                        do_tap(ygt[:], ("ygt",))
                        return
                    for grp in range(2):
                        ops = []
                        for k in range(8):
                            dcx = grp * 8 + k
                            ops.append(lambda e, s=ygt[:, dcx * 128:(dcx + 1) * 128],
                                       o=PT[:, k * 128:(k + 1) * 128]:
                                       e.transpose(out=o, in_=s, identity=idb_sb[:]))
                        p.op("tensor", ops, reads=("ygt", "idb"), writes=("PT",))
                        p.op("vector", lambda e, c=c, grp=grp: e.tensor_copy(
                            out=gbuf[:, 0:4096].rearrange("p (d q) -> p d q", q=256)
                            [:, grp * 8:(grp + 1) * 8, (c % 2) * 128:(c % 2) * 128 + 128],
                            in_=PT[:].rearrange("p (d q) -> p d q", q=128)),
                            reads=("PT",), writes=("gsc",))
                    p.op("vector", lambda e, c=c: e.tensor_tensor(
                        out=xs_sb[:].rearrange("p (h q) -> p h q", q=64),
                        in0=xtms.rearrange("p (h q) -> p h q", q=64),
                        in1=vT[:, c * 32:(c + 1) * 32].unsqueeze(2)
                        .to_broadcast([128, 32, 64]), op=OP.mult),
                        reads=("xtmc%d" % sl, "vTT"), writes=("xs",))
                    p.op("vector", lambda e, c=c: e.tensor_tensor(
                        out=gdthi, in0=idb_sb[0:32, 0:32],
                        in1=endshi[:, c:c + 1].to_broadcast([32, 32]),
                        op=OP.mult), reads=("idb", "ehi"), writes=("gdt",))
                    p.op("vector", lambda e, c=c: e.tensor_tensor(
                        out=gdtlo, in0=idb_sb[0:32, 0:32],
                        in1=endslo[:, c:c + 1].to_broadcast([32, 32]),
                        op=OP.mult), reads=("idb", "elo"), writes=("gdt",))
                    p.op("tensor",
                         [mmf(onesb_sb[0:32, :], gdthi, P3[:, 128:160],
                              True, False),
                          mmf(onesb_sb[0:32, :], gdtlo, P3[:, 128:160],
                              False, True)],
                         reads=("gdt", "onesb"), writes=("P3",))
                    p.op("scalar", lambda e: e.activation(
                        out=gamrep[:], in_=P3[:, 128:160], func=AF.Exp),
                        reads=("P3",), writes=("gamrep",))
                    if tap == "tgam%d" % c:
                        do_tap(gamrep[:], ("gamrep",))
                    p.op("gpsimd", lambda e: e.tensor_tensor(
                        out=rst[:].rearrange("p (h q) -> p h q", q=64),
                        in0=rst[:].rearrange("p (h q) -> p h q", q=64),
                        in1=gamrep[:].unsqueeze(2).to_broadcast([128, 32, 64]),
                        op=OP.mult), reads=("rst", "gamrep"), writes=("rst",))
                    for s2 in range(2):
                        ops = [mmf(btm[:, c * 128:(c + 1) * 128],
                                   xs_sb[:, s2 * 1024 + s4 * 512:
                                         s2 * 1024 + (s4 + 1) * 512],
                                   P2[:, s4 * 512:(s4 + 1) * 512], True, True)
                               for s4 in range(2)]
                        p.op("tensor", ops, reads=("btm", "xs"), writes=("P2",))
                        p.op("vector", lambda e, s2=s2: e.tensor_tensor(
                            out=rst[:, s2 * 1024:(s2 + 1) * 1024],
                            in0=rst[:, s2 * 1024:(s2 + 1) * 1024],
                            in1=P2[:], op=OP.add),
                            reads=("P2", "rst"), writes=("rst",))
                    p.op("vector", lambda e: e.tensor_copy(out=rstb[:], in_=rst[:]),
                         reads=("rst",), writes=("rstb",))
                    if c % 2 == 1:
                        gidx = c // 2
                        wsrc = woutb_d.rearrange("(g p) n -> p g n", p=128)
                        for och in range(8):
                            q = och % 4
                            osl = slice(och * 128, (och + 1) * 128)
                            if och % 2 == 0:
                                p.dma("scalar",
                                      wob[:].rearrange("p (g n) -> p g n", n=128),
                                      wsrc[:, :, osl],
                                      reads=("woutbd",), writes=("wo0",))
                                wt = [wob[:, dcg * 128:(dcg + 1) * 128]
                                      for dcg in range(16)]
                                rds = ("wo0",)
                            else:
                                rex = ("aTlo", "cTlo", "aThi", "cThi") \
                                    if c == 1 and och == 1 else ()
                                p.dma("scalar",
                                      wob2a.rearrange("p (g n) -> p g n", n=128),
                                      wsrc[:, 0:8, osl],
                                      reads=("woutbd",) + rex, writes=("wo1",))
                                p.dma("scalar",
                                      wob2b.rearrange("p (g n) -> p g n", n=128),
                                      wsrc[:, 8:16, osl],
                                      reads=("woutbd",), writes=("wo1",))
                                wt = [(wob2a if dcg < 8 else wob2b)
                                      [:, (dcg % 8) * 128:(dcg % 8) * 128 + 128]
                                      for dcg in range(16)]
                                rds = ("wo1",)
                            ops = [mmf(wt[dcg],
                                       gbuf[:, dcg * 256:(dcg + 1) * 256],
                                       P1[:, q * 512:q * 512 + 256],
                                       dcg == 0, dcg == 15)
                                   for dcg in range(16)]
                            p.op("tensor", ops, reads=rds + ("gsc",),
                                 writes=(P1r[q],))
                            ssl = xstage[:, (och % 2) * NT: (och % 2) * NT + 256]
                            xres = "xs%d" % (och % 2)
                            dsl = out_t[och * 128:(och + 1) * 128,
                                        gidx * 256:(gidx + 1) * 256]
                            p.dma("sync", ssl, dsl, reads=("xd",), writes=(xres,))
                            p.op("vector", lambda e, x=ssl, q=q, o=och:
                                 e.scalar_tensor_tensor(
                                     out=x, in0=P1[:, q * 512:q * 512 + 256],
                                     scalar=ccol[:, o:o + 1], in1=x,
                                     op0=OP.add, op1=OP.add),
                                 reads=(P1r[q], "ccol", xres), writes=(xres,))
                            p.dma("gpsimd", dsl, ssl, reads=(xres,), writes=("xd",))
            for k in range(16):
                dc, H = k // 2, k % 2
                ssl = xstage[:, (k % 2) * NT:(k % 2 + 1) * NT]
                xres = "xs%d" % (k % 2)
                p.dma("sync", ssl, out_t[dc * 128:(dc + 1) * 128, H * NT:(H + 1) * NT],
                      reads=("xd",), writes=(xres,))
                p.op("vector", lambda e, x=ssl,
                     d=xb[:, dc * T + H * NT: dc * T + H * NT + NT]:
                     e.tensor_copy(out=d, in_=x), reads=(xres,), writes=("xb",))
            if tap == "xmix":
                do_tap(xb[:], ("xb",))
                return

    emit_ffn(0)
    if not p.dead:
        p.barrier()
    emit_mixer()
    if not p.dead:
        p.barrier()
    emit_ffn(1)
    p.final_wait("gpsimd", ("xd",))
    p.op("gpsimd", lambda e: e.nop())
    p.realize()
    st.close()
    return nc


def host_inputs(inputs):
    u = np.asarray(inputs["L0_attn_u"])
    w_in = np.ascontiguousarray(inputs["L0_attn_W_in"], np.float32)
    conv_w = np.asarray(inputs["L0_attn_conv_w"])
    conv_b = np.asarray(inputs["L0_attn_conv_b"])
    dt_bias = np.asarray(inputs["L0_attn_dt_bias"]).reshape(32, 1).astype(np.float32)
    a_log = np.asarray(inputs["L0_attn_A_log"]).reshape(32, 1).astype(np.float32)
    norm_w = np.asarray(inputs["L0_attn_norm_w"])
    norm_b = np.asarray(inputs["L0_attn_norm_b"])
    w_out = np.ascontiguousarray(inputs["L0_attn_W_out"], np.float32)
    qw = np.asarray(inputs["L0_ffn_qweights"])
    sc = np.asarray(inputs["L0_ffn_scales"])
    bi = np.asarray(inputs["L0_ffn_biases"], np.float32)
    la = np.asarray(inputs["L0_ffn_lora_a"])
    lb = np.asarray(inputs["L0_ffn_lora_b"])
    lnw = np.asarray(inputs["L0_ffn_ln_gamma"])
    lnb = np.asarray(inputs["L0_ffn_ln_beta"])

    wqt = np.ascontiguousarray(qw.transpose(0, 2, 1), np.int32)
    srt = np.ascontiguousarray(np.repeat(sc, 16, axis=2).transpose(0, 2, 1),
                               np.float32)
    biasc = np.ascontiguousarray(
        bi.reshape(6, 8, 128).transpose(2, 0, 1).reshape(128, 48), np.float32)
    lat = np.ascontiguousarray(
        la.transpose(0, 2, 1).reshape(6, 8, 128, 32).transpose(2, 0, 1, 3)
        .reshape(128, 6 * 8 * 32), np.float32)
    lbt = np.ascontiguousarray(
        lb.transpose(0, 2, 1).transpose(1, 0, 2).reshape(32, 6 * D), np.float32)
    lnw3 = np.ascontiguousarray(
        lnw.reshape(3, 8, 128).transpose(2, 0, 1).reshape(128, 24), np.float32)
    lnb3 = np.ascontiguousarray(
        lnb.reshape(3, 8, 128).transpose(2, 0, 1).reshape(128, 24), np.float32)
    nwc = np.ascontiguousarray(norm_w.reshape(16, 128).T, np.float32)
    nbc = np.ascontiguousarray(norm_b.reshape(16, 128).T, np.float32)
    cb = np.ascontiguousarray(conv_b.reshape(18, 128).T, np.float32)
    cd = np.zeros((128, 4 * 18 * 128), np.float32)
    for i in range(4):
        for c in range(18):
            blk = cd[:, (i * 18 + c) * 128:(i * 18 + c + 1) * 128]
            np.fill_diagonal(blk, conv_w[c * 128:(c + 1) * 128, i])
    identb = np.eye(128, dtype=ml_dtypes.bfloat16)
    identf = np.eye(128, dtype=np.float32)
    tri = np.triu(np.ones((128, 128), np.float32)).copy()
    onesbf = np.ones((128, 128), dtype=ml_dtypes.bfloat16)
    onesf = np.ones((128, 512), np.float32)

    shared = dict(win=w_in, wqt=wqt, srt=srt, biasc=biasc, lat=lat, lbt=lbt,
                  lnw3=lnw3, lnb3=lnb3, wout=w_out, nwc=nwc, nbc=nbc,
                  cdiag=cd, cbias=cb, dtb=dt_bias, alog=a_log,
                  identb=identb, identf=identf, trimask=tri,
                  onesbf=onesbf, onesf=onesf)
    in_maps = []
    for b in range(8):
        m = dict(shared)
        m["xin"] = np.ascontiguousarray(u[b].T, np.float32)
        in_maps.append(m)
    return in_maps


def kernel(**inputs):
    in_maps = host_inputs(inputs)
    nc = build_program()
    res = run_bass_kernel_spmd(nc, in_maps, core_ids=list(range(8)))
    out = np.stack([np.asarray(r["out"]).T for r in res.results])
    return out.astype(np.float32)


# version 15 (final)
# speedup vs baseline: 1.1606x; 1.0035x over previous
import numpy as np
import ml_dtypes
import concourse.bass as bass
import concourse.mybir as mybir
from concourse.bass_utils import run_bass_kernel_spmd
from concourse.alu_op_type import AluOpType
from contextlib import ExitStack

F32 = mybir.dt.float32
BF16 = mybir.dt.bfloat16
I32 = mybir.dt.int32
AF = mybir.ActivationFunctionType
OP = AluOpType

T = 2048
D = 1024
DI = 2048
DCONV = 2304
DPROJ = 4384
CS = 128
NCH = T // CS
NT = 1024
EPS = 1e-5

ENGS = ("sync", "vector", "scalar", "tensor", "gpsimd")
DMA_SEM = {"sync": "dSY", "vector": "dV", "scalar": "dSC", "tensor": "dT",
           "gpsimd": "dG"}


class Prog:

    def __init__(self, nc, st):
        self.nc = nc
        self.st = st
        self.items = {e: [] for e in ENGS}
        self.semn = {}
        self.semh = {}
        for e in ENGS:
            self.semh[e] = st.enter_context(nc.semaphore("s_" + e))
            self.semn[e] = 0
        self.res = {}
        self.waited = {e: {} for e in ENGS}
        self.nops = 0
        self.nsem = len(ENGS)
        self.dead = False
        self.psum_res = set()

    SEM_GROUP = {}

    def dma_sem(self, eng, res):
        res = self.SEM_GROUP.get(res, res)
        nm = "d|%s|%s" % (eng, res)
        if nm not in self.semh:
            self.semh[nm] = self.st.enter_context(
                self.nc.semaphore("d_%s_%s" % (eng, res)))
            self.semn[nm] = 0
            self.nsem += 1
        return nm

    def _r(self, name):
        return self.res.setdefault(name, {"w": None, "r": []})

    def _need(self, reads, writes):
        needed = {}

        def add(sn, tk):
            if sn.startswith("d|"):
                tk = self.semn[sn]
            needed[sn] = max(needed.get(sn, 0), tk)

        for r in reads:
            w = self._r(r)["w"]
            if w is not None:
                add(*w)
        for r in writes:
            rec = self._r(r)
            if rec["w"] is not None:
                add(*rec["w"])
            for (sn, tk) in rec["r"]:
                add(sn, tk)
        return needed

    def _waits(self, eng, needed):
        wt = self.waited[eng]
        for sn, tk in sorted(needed.items()):
            if wt.get(sn, 0) >= tk:
                continue
            wt[sn] = tk
            h = self.semh[sn]
            self.items[eng].append(lambda e, h=h, v=tk: e.wait_ge(h, v))

    def _update(self, reads, writes, ident):
        for r in reads:
            self._r(r)["r"].append(ident)
        for r in writes:
            rec = self._r(r)
            rec["w"] = ident
            rec["r"] = []

    def op(self, eng, fns, reads=(), writes=()):
        if self.dead:
            return ("sync", 0)
        if callable(fns):
            fns = [fns]
        pr = tuple(r for r in reads if r in self.psum_res)
        if pr:
            reads = tuple(r for r in reads if r not in self.psum_res)
            writes = tuple(writes) + pr
        self._waits(eng, self._need(reads, writes))
        self.semn[eng] += 1
        tok = self.semn[eng]
        h = self.semh[eng]
        for f in fns[:-1]:
            self.items[eng].append(f)
        self.items[eng].append(lambda e, f=fns[-1], h=h: f(e).then_inc(h, 1))
        self.nops += len(fns)
        self._update(reads, writes, (eng, tok))
        return (eng, tok)

    def dma(self, eng, dst, src, reads=(), writes=(), slow=False):
        if self.dead:
            return ("sync", 0)
        assert len(writes) == 1
        self._waits(eng, self._need(reads, writes))
        dname = self.dma_sem(eng, writes[0])
        self.semn[dname] += 16
        tok = self.semn[dname]
        h = self.semh[dname]
        self.items[eng].append(
            lambda e, o=dst, i=src, h=h, sl=slow: e.dma_start(
                out=o, in_=i, allow_slow_non_contiguous=sl).then_inc(h, 16))
        self.nops += 1
        self._update(reads, writes, (dname, tok))
        return (dname, tok)

    def final_wait(self, eng, reads):
        self._waits(eng, self._need(reads, ()))

    def barrier(self):
        if self.dead:
            return
        if not hasattr(self, "bsem"):
            self.bsem = self.st.enter_context(self.nc.semaphore("barrier"))
            self.bn = 0
        for sn, cnt in sorted(self.semn.items()):
            if sn == "sync" or cnt == 0:
                continue
            if self.waited["sync"].get(sn, 0) >= cnt:
                continue
            self.waited["sync"][sn] = cnt
            h = self.semh[sn]
            self.items["sync"].append(lambda e, h=h, v=cnt: e.wait_ge(h, v))
        self.bn += 1
        bh = self.bsem
        bv = self.bn
        self.items["sync"].append(lambda e, h=bh, v=bv: e.nop().then_inc(h, 1))
        self.semn["sync"] += 1
        snap = dict(self.semn)
        for eng in ENGS:
            if eng != "sync":
                self.items[eng].append(lambda e, h=bh, v=bv: e.wait_ge(h, v))
            self.waited[eng] = dict(snap)

    def realize(self):
        with self.nc.Block() as block:
            for eng in ENGS:
                items = self.items[eng]
                if items:
                    getattr(block, eng)(lambda e, it=items: [f(e) for f in it])



_G = {}
for _i in range(16):
    _G["xtm%d" % _i] = "xtmg"
    _G["wo_%d" % _i] = "wog"
for _sl in range(2):
    for _dc in range(8):
        _G["wb%d_%d" % (_sl, _dc)] = "wbg%d" % _sl
for _w in range(6):
    _G["wdq%d" % _w] = "wdqg"
for _nm in ("idb", "idf", "onesb", "onesf", "tri", "lnw", "lnb", "bcol",
            "la_f", "lb_f", "cdf", "cb", "nw", "nb", "dtbr", "alogr"):
    _G[_nm] = "constg"
for _nm in ("hidacum", "lodacum", "hidcombo", "lodcombo", "acumd", "combod"):
    _G[_nm] = "hilog"
for _nm in ("qst", "sst"):
    _G[_nm] = "dqg"
Prog.SEM_GROUP = _G


def build_program(tap=None):
    nc = bass.Bass("TRN2", target_bir_lowering=False)
    st = ExitStack()

    def din(name, shape, dt=F32):
        return nc.dram_tensor(name, shape, dt, kind="ExternalInput").ap()

    xin = din("xin", [D, T])
    win = din("win", [D, DPROJ])
    wqt = din("wqt", [6, D, D], I32)
    srt = din("srt", [6, D, D])
    biasc = din("biasc", [128, 48])
    lat = din("lat", [128, 6 * 8 * 32])
    lbt = din("lbt", [32, 6 * D])
    lnw3 = din("lnw3", [128, 24])
    lnb3 = din("lnb3", [128, 24])
    wout = din("wout", [DI, D])
    nwc = din("nwc", [128, 16])
    nbc = din("nbc", [128, 16])
    cdiag = din("cdiag", [128, 4 * 18 * 128])
    cbias = din("cbias", [128, 18])
    dtb = din("dtb", [32, 1])
    alog = din("alog", [32, 1])
    identb = din("identb", [128, 128], BF16)
    identf = din("identf", [128, 128])
    trimask = din("trimask", [128, 128])
    onesbf = din("onesbf", [128, 128], BF16)
    onesf = din("onesf", [128, 512])

    out_t = nc.dram_tensor("out", [D, T], F32, kind="ExternalOutput").ap()
    wdq = nc.dram_tensor("wdq", [6, D, D], BF16, kind="Internal").ap()
    winb_d = nc.dram_tensor("winb_d", [D, DPROJ], BF16, kind="Internal").ap()
    woutb_d = nc.dram_tensor("woutb_d", [DI, D], BF16, kind="Internal").ap()
    zsil_d = nc.dram_tensor("zsil_d", [T, DI], BF16, kind="Internal").ap()
    xbcpre_d = nc.dram_tensor("xbcpre_d", [DCONV, T], BF16, kind="Internal").ap()
    xtm_d = nc.dram_tensor("xtm_d", [T, DI], BF16, kind="Internal").ap()
    acum_d = nc.dram_tensor("acum_d", [32, T], F32, kind="Internal").ap()
    combo_d = nc.dram_tensor("combo_d", [32, T], F32, kind="Internal").ap()
    acumhi_d = nc.dram_tensor("acumhi_d", [32, T], F32, kind="Internal").ap()
    acumlo_d = nc.dram_tensor("acumlo_d", [32, T], F32, kind="Internal").ap()
    gend_d = nc.dram_tensor("gend_d", [NCH, 32], F32, kind="Internal").ap()

    p = Prog(nc, st)
    TAP = {"done": False}

    def sb(stk, name, shape, dt=F32):
        return stk.enter_context(nc.sbuf_tensor(name, shape, dt))

    def psum(stk, name, shape, dt=F32):
        return stk.enter_context(nc.psum_tensor(name, shape, dt))

    xb = sb(st, "xb", [128, 8 * T], BF16)
    hb = sb(st, "hb", [128, 8 * T], BF16)
    la_sb = sb(st, "la_sb", [128, 6 * 8 * 32], BF16)
    lb_sb = sb(st, "lb_sb", [32, 6 * D], BF16)
    mid_sb = sb(st, "mid_sb", [32, NT], BF16)
    bc_sb = sb(st, "bc_sb", [128, 48])
    lnw_sb = sb(st, "lnw_sb", [128, 24])
    lnb_sb = sb(st, "lnb_sb", [128, 24])
    idb_sb = sb(st, "idb_sb", [128, 128], BF16)
    idf_sb = sb(st, "idf_sb", [128, 128])
    onesb_sb = sb(st, "onesb_sb", [128, 128], BF16)
    onesf_sb = sb(st, "onesf_sb", [128, 512])
    tri_sb = sb(st, "tri_sb", [128, 128])
    xstage = sb(st, "xstage", [128, 2 * NT])

    def mmf(lhsT, rhs, out, start, stop):
        return lambda e, a=lhsT, b=rhs, o=out, s1=start, s2=stop: \
            e.matmul(out=o, lhsT=a, rhs=b, start=s1, stop=s2)

    def do_tap(src_ap, reads):
        TAP["done"] = True
        tp = nc.dram_tensor("tapo", list(src_ap.shape), src_ap.dtype,
                            kind="ExternalOutput").ap()
        p.dma("gpsimd", tp, src_ap, reads=reads, writes=("tapped",))
        p.dead = True

    with ExitStack() as st0:
        la_f = sb(st0, "la_f", [128, 6 * 8 * 32])
        lb_f = sb(st0, "lb_f", [32, 6 * D])
        for dst, src, rn in [
                (idb_sb[:], identb, "idb"), (idf_sb[:], identf, "idf"),
                (onesb_sb[:], onesbf, "onesb"), (onesf_sb[:], onesf, "onesf"),
                (tri_sb[:], trimask, "tri"), (lnw_sb[:], lnw3, "lnw"),
                (lnb_sb[:], lnb3, "lnb"), (bc_sb[:], biasc, "bcol"),
                (la_f[:], lat, "la_f"), (lb_f[:], lbt, "lb_f")]:
            p.dma("sync", dst, src, writes=(rn,))
        p.op("vector", lambda e: e.tensor_copy(out=la_sb[:], in_=la_f[:]),
             reads=("la_f",), writes=("la",))
        p.op("vector", lambda e: e.tensor_copy(out=lb_sb[:], in_=lb_f[:]),
             reads=("lb_f",), writes=("lb",))
        for k in range(16):
            sl = xstage[:, (k % 2) * NT:(k % 2 + 1) * NT]
            dc, Hh = k // 2, k % 2
            src = xin[dc * 128:(dc + 1) * 128, Hh * NT:(Hh + 1) * NT]
            dst = out_t[dc * 128:(dc + 1) * 128, Hh * NT:(Hh + 1) * NT]
            p.dma("sync", sl, src, writes=("xs%d" % (k % 2),))
            p.op("vector", lambda e, s=sl, d=xb[:, dc * T + Hh * NT: dc * T + Hh * NT + NT]:
                 e.tensor_copy(out=d, in_=s), reads=("xs%d" % (k % 2),), writes=("xb",))
            p.dma("gpsimd", dst, sl, reads=("xs%d" % (k % 2),), writes=("xd",))

    def emit_ffn(call):
        with ExitStack() as fst:
            gbuf = sb(fst, "gbuf%d" % call, [128, 8 * NT], BF16)
            yraw = sb(fst, "yraw%d" % call, [128, 8 * NT], BF16)
            wbuf = sb(fst, "wbuf%d" % call, [128, 2 * 8 * D], BF16)
            qi_sb = sb(fst, "qi%d" % call, [128, 2 * D], I32)
            sr_sb = sb(fst, "sr%d" % call, [128, 2 * D])
            qf_sb = sb(fst, "qf%d" % call, [128, D])
            ps_y = [psum(fst, "fy0_%d" % call, [128, NT]),
                    psum(fst, "fy1_%d" % call, [128, NT])]
            ps_s0 = psum(fst, "fs0_%d" % call, [128, NT])
            ps_s1 = psum(fst, "fs1_%d" % call, [128, NT])
            p.psum_res |= {"fy0", "fy1", "fs0", "fs1"}
            y2 = sb(fst, "y2_%d" % call, [128, NT], BF16)
            y2b = sb(fst, "y2b_%d" % call, [128, NT], BF16)
            y2s = [y2, y2b]
            r_rep = sb(fst, "rrep%d" % call, [128, NT], BF16)
            nmr_rep = sb(fst, "nmrep%d" % call, [128, NT], BF16)
            mu_rep = sb(fst, "murep%d" % call, [128, NT])
            m2_rep = sb(fst, "m2rep%d" % call, [128, NT])
            var_rep = sb(fst, "varep%d" % call, [128, NT])

            def stats_rows(ps_sum, rs_sum, ps_sq, rs_sq, nfeat):
                inv = 1.0 / nfeat
                p.op("vector", lambda e: e.tensor_scalar(
                    out=mu_rep[:], in0=ps_sum, scalar1=-inv, scalar2=None,
                    op0=OP.mult), reads=rs_sum, writes=("mu",))
                p.op("vector", lambda e: e.tensor_scalar(
                    out=var_rep[:], in0=ps_sq, scalar1=inv, scalar2=float(EPS),
                    op0=OP.mult, op1=OP.add), reads=rs_sq, writes=("var",))
                if tap == "sA" and not TAP["done"]:
                    do_tap(var_rep[:], ("var",))
                    return
                p.op("scalar", lambda e: e.activation(
                    out=m2_rep[:], in_=ps_sum, func=AF.Square, scale=inv),
                    reads=rs_sum, writes=("m2",))
                p.op("vector", lambda e: e.tensor_tensor(
                    out=var_rep[:], in0=var_rep[:], in1=m2_rep[:], op=OP.subtract),
                    reads=("var", "m2"), writes=("var",))
                if tap == "sB" and not TAP["done"]:
                    do_tap(var_rep[:], ("var",))
                    return
                p.op("scalar", lambda e: e.activation(
                    out=m2_rep[:], in_=var_rep[:], func=AF.Ln),
                    reads=("var",), writes=("m2",))
                p.op("scalar", lambda e: e.activation(
                    out=r_rep[:], in_=m2_rep[:], func=AF.Exp, scale=-0.5),
                    reads=("m2",), writes=("rr",))
                p.op("vector", lambda e: e.tensor_tensor(
                    out=nmr_rep[:], in0=mu_rep[:], in1=r_rep[:], op=OP.mult),
                    reads=("mu", "rr"), writes=("nmr",))
                if tap == "sC" and not TAP["done"]:
                    do_tap(nmr_rep[:], ("nmr",))
                    return

            def top_ln_to(dres):
                for H in range(2):
                    t0 = H * NT
                    for dc in range(8):
                        xs = xb[:, dc * T + t0: dc * T + t0 + NT]
                        p.op("tensor",
                             [mmf(onesb_sb[:], xs[:, 0:512], ps_s0[:, 0:512], dc == 0, dc == 7),
                              mmf(onesb_sb[:], xs[:, 512:NT], ps_s0[:, 512:NT], dc == 0, dc == 7)],
                             reads=("onesb", "xb"), writes=("fs0",))

                    def sq_mms(dc):
                        y2c = y2s[dc % 2]
                        p.op("tensor",
                             [mmf(onesb_sb[:], y2c[:, 0:512], ps_s1[:, 0:512], dc == 0, dc == 7),
                              mmf(onesb_sb[:], y2c[:, 512:NT], ps_s1[:, 512:NT], dc == 0, dc == 7)],
                             reads=("onesb", "y2%d" % (dc % 2)), writes=("fs1",))

                    for dc in range(8):
                        xs = xb[:, dc * T + t0: dc * T + t0 + NT]
                        p.op("scalar", lambda e, s=xs, d=y2s[dc % 2]: e.activation(
                            out=d[:], in_=s, func=AF.Square), reads=("xb",),
                            writes=("y2%d" % (dc % 2),))
                        if dc >= 1:
                            sq_mms(dc - 1)
                    sq_mms(7)
                    stats_rows(ps_s0[:], ("fs0",), ps_s1[:], ("fs1",), D)
                    for dc in range(8):
                        xs = xb[:, dc * T + t0: dc * T + t0 + NT]
                        hs = hb[:, dc * T + t0: dc * T + t0 + NT]
                        y2c = y2s[dc % 2]
                        yr2 = "y2%d" % (dc % 2)
                        p.op("vector", lambda e, a=xs, d=y2c: e.tensor_tensor(
                            out=d[:], in0=a, in1=r_rep[:], op=OP.mult),
                            reads=("xb", "rr"), writes=(yr2,))
                        p.op("vector", lambda e, d=hs, sc=y2c: e.tensor_tensor(
                            out=d, in0=sc[:], in1=nmr_rep[:], op=OP.add),
                            reads=(yr2, "nmr"), writes=(dres,))

            def dequant_weight(w):
                slot = w % 2
                for dc in range(8):
                    par = dc % 2
                    qs = qi_sb[:, par * D:(par + 1) * D]
                    srs = sr_sb[:, par * D:(par + 1) * D]
                    p.dma("sync", qs, wqt[w, dc * 128:(dc + 1) * 128, :],
                          writes=("qst%d" % par,))
                    p.dma("sync", srs, srt[w, dc * 128:(dc + 1) * 128, :],
                          writes=("sst%d" % par,))
                    p.op("scalar", lambda e, q=qs: e.activation(
                        out=qf_sb[:], in_=q, func=AF.Copy, bias=-1.0,
                        scale=1.0 / 7.5), reads=("qst%d" % par,), writes=("qf",))
                    dst = wbuf[:, slot * 8 * D + dc * D: slot * 8 * D + (dc + 1) * D]
                    p.op("vector", lambda e, d=dst, s=srs: e.tensor_tensor(
                        out=d, in0=qf_sb[:], in1=s, op=OP.mult),
                        reads=("qf", "sst%d" % par), writes=("wb%d_%d" % (slot, dc),))
                    p.dma("gpsimd", wdq[w, dc * 128:(dc + 1) * 128, :], dst,
                          reads=("wb%d_%d" % (slot, dc),), writes=("wdq%d" % w,))

            def stream_weight(w):
                slot = w % 2
                for dc in range(8):
                    dst = wbuf[:, slot * 8 * D + dc * D: slot * 8 * D + (dc + 1) * D]
                    p.dma("scalar", dst, wdq[w, dc * 128:(dc + 1) * 128, :],
                          reads=("wdq%d" % w,), writes=("wb%d_%d" % (slot, dc),))

            def lora_mid(widx, rhs_sl, rhs_res):
                ops = []
                for dc in range(8):
                    la = la_sb[:, (widx * 8 + dc) * 32:(widx * 8 + dc + 1) * 32]
                    ops.append(mmf(la, rhs_sl(dc)[:, 0:512],
                                   ps_s0[0:32, 0:512], dc == 0, dc == 7))
                    ops.append(mmf(la, rhs_sl(dc)[:, 512:NT],
                                   ps_s0[0:32, 512:NT], dc == 0, dc == 7))
                p.op("tensor", ops, reads=("la",) + rhs_res, writes=("fs0",))
                p.op("scalar", lambda e: e.copy(out=mid_sb[:], in_=ps_s0[0:32, :]),
                     reads=("fs0",), writes=("mid",))

            def qlora_mms(widx, rhs_sl, rhs_res, och, pst, pres):
                slot = widx % 2
                ops = []
                rds = {"mid"} | set(rhs_res)
                for dc in range(8):
                    lh = wbuf[:, slot * 8 * D + dc * D + och * 128:
                              slot * 8 * D + dc * D + och * 128 + 128]
                    rds.add("wb%d_%d" % (slot, dc))
                    ops.append(mmf(lh, rhs_sl(dc)[:, 0:512], pst[:, 0:512], dc == 0, False))
                    ops.append(mmf(lh, rhs_sl(dc)[:, 512:NT], pst[:, 512:NT], dc == 0, False))
                lbs = lb_sb[:, widx * D + och * 128: widx * D + och * 128 + 128]
                ops.append(mmf(lbs, mid_sb[:, 0:512], pst[:, 0:512], False, True))
                ops.append(mmf(lbs, mid_sb[:, 512:NT], pst[:, 512:NT], False, True))
                p.op("tensor", ops, reads=tuple(rds) + ("lb",), writes=(pres,))

            top_ln_to("h")
            if tap == "h0_%d" % call:
                do_tap(hb[:], ("h",))
                return
            (dequant_weight if call == 0 else stream_weight)(0)
            (dequant_weight if call == 0 else stream_weight)(1)
            if tap == "wb_%d" % call:
                do_tap(wbuf[:], tuple("wb%d_%d" % (sl_, dc_) for sl_ in range(2)
                                      for dc_ in range(8)))
                return
            for blk in range(3):
                i0, i1 = 2 * blk, 2 * blk + 1
                for H in range(2):
                    t0 = H * NT
                    hsl = lambda dc, t0=t0: hb[:, dc * T + t0: dc * T + t0 + NT]
                    gsl = lambda dc: gbuf[:, dc * NT:(dc + 1) * NT]
                    gres = tuple("g%d" % i for i in range(8))
                    lora_mid(i0, hsl, ("h",))
                    if tap == "mid_%d_%d_%d" % (call, blk, H):
                        do_tap(mid_sb[:], ("mid",))
                        return

                    def emit_stats_mms(och):
                        ysl = yraw[:, och * NT:(och + 1) * NT]
                        y2c = y2s[och % 2]
                        p.op("tensor",
                             [mmf(onesb_sb[:], ysl[:, 0:512], ps_s0[:, 0:512], och == 0, och == 7),
                              mmf(onesb_sb[:], ysl[:, 512:NT], ps_s0[:, 512:NT], och == 0, och == 7)],
                             reads=("onesb", "yr%d" % och), writes=("fs0",))
                        p.op("tensor",
                             [mmf(onesb_sb[:], y2c[:, 0:512], ps_s1[:, 0:512], och == 0, och == 7),
                              mmf(onesb_sb[:], y2c[:, 512:NT], ps_s1[:, 512:NT], och == 0, och == 7)],
                             reads=("onesb", "y2%d" % (och % 2)), writes=("fs1",))

                    for och in range(8):
                        pr = "fy%d" % (och % 2)
                        qlora_mms(i0, hsl, ("h",), och, ps_y[och % 2], pr)
                        bcol = bc_sb[:, i0 * 8 + och: i0 * 8 + och + 1]
                        ysl = yraw[:, och * NT:(och + 1) * NT]
                        p.op("scalar", lambda e, d=ysl, s=ps_y[och % 2], b=bcol:
                             e.activation(out=d, in_=s[:], func=AF.Identity,
                                          bias=b, scale=1.0),
                             reads=(pr, "bcol"), writes=("yr%d" % och,))
                        p.op("scalar", lambda e, s=ps_y[och % 2], b=bcol,
                             d=y2s[och % 2]:
                             e.activation(out=d[:], in_=s[:], func=AF.Square,
                                          bias=b, scale=1.0),
                             reads=(pr, "bcol"), writes=("y2%d" % (och % 2),))
                        if och >= 1:
                            emit_stats_mms(och - 1)
                    emit_stats_mms(7)
                    if tap == "yraw_%d_%d_%d" % (call, blk, H):
                        do_tap(yraw[:], tuple("yr%d" % i for i in range(8)))
                        return
                    stats_rows(ps_s0[:], ("fs0",), ps_s1[:], ("fs1",), D)
                    if tap == "pre" and not TAP["done"]:
                        do_tap(y2[:], ("y2",))
                        return
                    for och in range(8):
                        ysl = yraw[:, och * NT:(och + 1) * NT]
                        p.op("vector", lambda e, a=ysl: e.tensor_tensor(
                            out=y2[:], in0=a, in1=r_rep[:], op=OP.mult),
                            reads=("yr%d" % och, "rr"), writes=("y2",))
                        if tap == "ap%d" % och and not TAP["done"]:
                            do_tap(y2[:], ("y2",))
                            return
                        p.op("vector", lambda e: e.tensor_tensor(
                            out=y2[:], in0=y2[:], in1=nmr_rep[:], op=OP.add),
                            reads=("y2", "nmr"), writes=("y2",))
                        if tap == "aq%d" % och and not TAP["done"]:
                            do_tap(y2[:], ("y2",))
                            return
                        gcol = blk * 8 + och
                        p.op("scalar", lambda e, d=gsl(och), gc=gcol: e.activation(
                            out=d, in_=y2[:], func=AF.Gelu,
                            bias=lnb_sb[:, gc:gc + 1], scale=lnw_sb[:, gc:gc + 1]),
                            reads=("y2", "lnw", "lnb"), writes=("g%d" % och,))
                        if tap == "ag%d" % och and not TAP["done"]:
                            do_tap(gbuf[:], ("g%d" % och,))
                            return
                    if tap == "g_%d_%d_%d" % (call, blk, H):
                        do_tap(gbuf[:], gres)
                        return
                    lora_mid(i1, gsl, gres)
                    for och in range(8):
                        pr = "fy%d" % (och % 2)
                        qlora_mms(i1, gsl, gres, och, ps_y[och % 2], pr)
                        bcol = bc_sb[:, i1 * 8 + och: i1 * 8 + och + 1]
                        hs = hsl(och)
                        p.op("vector", lambda e, d=hs, s=ps_y[och % 2], b=bcol:
                             e.scalar_tensor_tensor(out=d, in0=s[:], scalar=b,
                                                    in1=d, op0=OP.add, op1=OP.add),
                             reads=(pr, "bcol", "h"), writes=("h",))
                if tap == "hblk_%d_%d" % (call, blk):
                    do_tap(hb[:], ("h",))
                    return
                for wnext in (i0 + 2, i0 + 3):
                    if wnext <= 5:
                        (dequant_weight if call == 0 else stream_weight)(wnext)
            for k in range(16):
                dc, H = k // 2, k % 2
                sl = xstage[:, (k % 2) * NT:(k % 2 + 1) * NT]
                xres = "xs%d" % (k % 2)
                dsl = out_t[dc * 128:(dc + 1) * 128, H * NT:(H + 1) * NT]
                hs = hb[:, dc * T + H * NT: dc * T + H * NT + NT]
                p.dma("sync", sl, dsl, reads=("xd",), writes=(xres,))
                p.op("vector", lambda e, x=sl, h=hs: e.scalar_tensor_tensor(
                    out=x, in0=h, scalar=0.5, in1=x, op0=OP.mult, op1=OP.add),
                    reads=("h", xres), writes=(xres,))
                p.op("vector", lambda e, x=sl, d=xb[:, dc * T + H * NT: dc * T + H * NT + NT]:
                     e.tensor_copy(out=d, in_=x), reads=(xres,), writes=("xb",))
                p.dma("gpsimd", dsl, sl, reads=(xres,), writes=("xd",))
            if call == 0:
                top_ln_to("h")
                if tap == "u":
                    do_tap(hb[:], ("h",))

    def emit_mixer():
        with ExitStack() as mst:
            gbuf = sb(mst, "mgb", [128, 8 * NT], BF16)
            yraw = sb(mst, "myr", [128, T], BF16)
            P1 = psum(mst, "P1", [128, 2048])
            P2 = psum(mst, "P2", [128, 1024])
            P3 = psum(mst, "P3", [128, 512])
            PT = psum(mst, "PT", [128, 1024], BF16)
            p.psum_res |= {"P1q0", "P1q1", "P1q2", "P1q3", "P2", "P3", "PT"}
            P1r = ("P1q0", "P1q1", "P1q2", "P1q3")

            acum = sb(mst, "acum", [32, T])
            combo = sb(mst, "combo", [32, T])
            acumT = sb(mst, "acumT", [128, NCH * 32])
            comboT = None
            vT = sb(mst, "vT", [128, NCH * 32])
            comboT = sb(mst, "comboT", [128, NCH * 32])
            expAT = sb(mst, "expAT", [128, NCH * 32], BF16)
            gamrep = sb(mst, "gamrep", [128, 32])
            ends32 = sb(mst, "ends32", [32, NCH])
            endshi = sb(mst, "endshi", [32, NCH], BF16)
            endslo = sb(mst, "endslo", [32, NCH], BF16)
            bfm = sb(mst, "bfm", [128, T], BF16)
            cfm = sb(mst, "cfm", [128, T], BF16)
            btm = sb(mst, "btm", [128, NCH * 128], BF16)
            rst = sb(mst, "rst", [128, DI])
            rstb = sb(mst, "rstb", [128, DI], BF16)
            xs_sb = sb(mst, "xs_sb", [128, DI], BF16)
            gm_sb = sb(mst, "gm_sb", [128, 128], BF16)
            ygt = sb(mst, "ygt", [128, DI], BF16)
            ccol = sb(mst, "ccol", [128, 8])
            sumc = sb(mst, "sumc", [128, 1])
            sqc = sb(mst, "sqc", [128, 1])
            mucol = sb(mst, "mucol", [128, 1])
            m2col = sb(mst, "m2col", [128, 1])
            varcol = sb(mst, "varcol", [128, 1])
            rcol = sb(mst, "rcol", [128, 1])
            nmrcol = sb(mst, "nmrcol", [128, 1])
            wio = sb(mst, "wio", [128, 2048])
            grow = wio[32:33, 0:NCH * 32]
            wiob = wio[:].bitcast(BF16)
            PTf32 = PT[:].bitcast(F32)
            wob2a = acumT[:].bitcast(BF16)
            wob2b = comboT[:].bitcast(BF16)
            esb = gbuf[:, 4096:4096 + NT].bitcast(BF16)[:, 0:NT] \
                if False else gbuf[:, 4096:4096 + NT]
            esb2 = gbuf[:, 5120:5120 + NT]
            mn8 = gbuf[:, 6144:6144 + NT]
            gdthi = gbuf[0:32, 7168:7200]
            gdtlo = gbuf[0:32, 7200:7232]
            wzb = sb(mst, "wzb", [128, 8 * 512], BF16)
            wxb = sb(mst, "wxb", [128, 2 * 8 * 128], BF16)
            wdtb = sb(mst, "wdtb", [128, 8 * 32], BF16)
            wob = sb(mst, "wob", [128, 16 * 128], BF16)
            prebuf = sb(mst, "prebuf", [128, T + 3], BF16)
            aThi = prebuf[:, 0:NCH * 32]
            aTlo = prebuf[:, NCH * 32:2 * NCH * 32]
            cThi = prebuf[:, 2 * NCH * 32:3 * NCH * 32]
            cTlo = prebuf[:, 3 * NCH * 32:4 * NCH * 32]
            cdg = sb(mst, "cdg", [128, 2 * 4 * 128], BF16)
            cb_sb = sb(mst, "cb_sb", [128, 18])
            nw_sb = sb(mst, "nw_sb", [128, 16])
            nb_sb = sb(mst, "nb_sb", [128, 16])
            nbb_sb = sb(mst, "nbb_sb", [128, 16], BF16)
            dtb_sb = sb(mst, "dtb_sb", [32, 1])
            alog_sb = sb(mst, "alog_sb", [32, 1])
            acol = sb(mst, "acol", [32, 1])

            for dst, src, rn in [(cb_sb[:], cbias, "cb"), (nw_sb[:], nwc, "nw"),
                                 (nb_sb[:], nbc, "nb"), (dtb_sb[:], dtb, "dtbr"),
                                 (alog_sb[:], alog, "alogr")]:
                p.dma("sync", dst, src, writes=(rn,))

            for dc in range(8):
                for cg in range(5):
                    c0 = cg * 1024
                    cw = min(1024, DPROJ - c0)
                    p.dma("sync", wio[:, 0:cw], win[dc * 128:(dc + 1) * 128, c0:c0 + cw],
                          writes=("wio",))
                    p.op("vector", lambda e, w=cw: e.tensor_copy(
                        out=gbuf[:, 0:w], in_=wio[:, 0:w]),
                        reads=("wio",), writes=("gsc",))
                    p.dma("gpsimd", winb_d[dc * 128:(dc + 1) * 128, c0:c0 + cw],
                          gbuf[:, 0:cw], reads=("gsc",), writes=("winb",))
            for dc in range(16):
                p.dma("sync", wio[:, 0:D], wout[dc * 128:(dc + 1) * 128, :],
                      writes=("wio",))
                p.op("vector", lambda e, c=nw_sb[:, dc:dc + 1]: e.tensor_scalar(
                    out=gbuf[:, 0:D], in0=wio[:, 0:D], scalar1=c, scalar2=None,
                    op0=OP.mult), reads=("wio", "nw"), writes=("gsc",))
                p.dma("gpsimd", woutb_d[dc * 128:(dc + 1) * 128, :], gbuf[:, 0:D],
                      reads=("gsc",), writes=("woutbd",))

            p.dma("scalar", wdtb[:].rearrange("p (g n) -> p g n", n=32),
                  winb_d.rearrange("(g p) n -> p g n", p=128)[:, :, 4352:4384],
                  reads=("winb",), writes=("wdtb",))
            if True:
                gf32 = gbuf[:].bitcast(F32)
                dte = gf32[0:32, 0:T]
                lndt = gf32[0:32, T:2 * T]
                for half in range(2):
                    ops = []
                    for dc in range(8):
                        for s4 in range(2):
                            ops.append(mmf(
                                wdtb[:, dc * 32:(dc + 1) * 32],
                                hb[:, dc * T + half * NT + s4 * 512:
                                   dc * T + half * NT + s4 * 512 + 512],
                                P2[0:32, s4 * 512:(s4 + 1) * 512], dc == 0, dc == 7))
                    p.op("tensor", ops, reads=("wdtb", "h"), writes=("P2",))
                    p.op("scalar", lambda e, h=half: e.copy(
                        out=acum[:, h * NT:(h + 1) * NT], in_=P2[0:32, :]),
                        reads=("P2",), writes=("acum",))
                p.op("scalar", lambda e: e.activation(
                    out=acol[:], in_=alog_sb[:], func=AF.Exp),
                    reads=("alogr",), writes=("acol",))
                p.op("vector", lambda e: e.tensor_scalar(
                    out=acol[:], in0=acol[:], scalar1=-1.0, scalar2=None, op0=OP.mult),
                    reads=("acol",), writes=("acol",))
                p.op("scalar", lambda e: e.activation(
                    out=dte[:], in_=acum[:], func=AF.Exp, bias=dtb_sb[:], scale=1.0),
                    reads=("acum", "dtbr", "gsc"), writes=("gsc",))
                p.op("scalar", lambda e: e.activation(
                    out=dte[:], in_=dte[:], func=AF.Ln, bias=1.0, scale=1.0),
                    reads=("gsc",), writes=("gsc",))
                p.op("scalar", lambda e: e.activation(
                    out=lndt[:], in_=dte[:], func=AF.Ln),
                    reads=("gsc",), writes=("gsc",))
                p.op("vector", lambda e: e.tensor_scalar(
                    out=combo[:], in0=dte[:], scalar1=acol[:], scalar2=None,
                    op0=OP.mult), reads=("gsc", "acol"), writes=("combo",))
                for c in range(NCH):
                    p.op("vector", lambda e, c=c: e.tensor_tensor_scan(
                        out=acum[:, c * CS:(c + 1) * CS],
                        data0=onesf_sb[0:32, 0:CS],
                        data1=combo[:, c * CS:(c + 1) * CS], initial=0.0,
                        op0=OP.mult, op1=OP.add),
                        reads=("combo", "onesf"), writes=("acum",))
                p.op("vector", lambda e: e.tensor_tensor(
                    out=combo[:], in0=lndt[:], in1=acum[:], op=OP.subtract),
                    reads=("gsc", "acum"), writes=("combo",))
                for c in range(NCH):
                    p.op("vector", lambda e, c=c: e.tensor_scalar(
                        out=lndt[:, c * CS:(c + 1) * CS],
                        in0=combo[:, c * CS:(c + 1) * CS],
                        scalar1=acum[:, c * CS + CS - 1: c * CS + CS],
                        scalar2=None, op0=OP.add),
                        reads=("combo", "acum", "gsc"), writes=("gsc",))
                p.op("scalar", lambda e: e.activation(
                    out=lndt[:], in_=lndt[:], func=AF.Exp),
                    reads=("gsc",), writes=("gsc",))
                ttmp = wio[64:96, 0:2048]
                for nmres, src, sres, dst in [("aT", acum, "acum", acumT),
                                              ("cT", combo, "combo", comboT),
                                              ("vTT", lndt, "gsc", vT)]:
                    p.op("vector", lambda e, srcl=src: e.transpose(
                        out=ttmp, in_=srcl[:]), reads=(sres, "wio"),
                        writes=("wio",))
                    for w in range(4):
                        p.dma("sync",
                              dst[w * 32:(w + 1) * 32, :]
                              .rearrange("p (c h) -> p c h", h=32),
                              ttmp.rearrange("p (c q) -> p c q", q=128)
                              [:, :, w * 32:w * 32 + 32],
                              reads=("wio",), writes=(nmres,))
                if tap == "aTT":
                    do_tap(acumT[:], ("aT",))
                p.op("scalar", lambda e: e.activation(
                    out=expAT[:], in_=acumT[:], func=AF.Exp),
                    reads=("aT",), writes=("expAT",))
            p.op("vector", lambda e: e.tensor_copy(
                out=ends32[:].unsqueeze(2),
                in_=acum[:].rearrange("p (c q) -> p c q", q=CS)[:, :, CS - 1:CS]),
                reads=("acum",), writes=("ends",))
            p.op("vector", lambda e: e.tensor_copy(out=endshi[:], in_=ends32[:]),
                 reads=("ends",), writes=("ehi",))
            p.op("vector", lambda e: e.tensor_copy(
                out=wio[0:32, 512:512 + NCH], in_=endshi[:]),
                reads=("ehi",), writes=("wio",))
            p.op("vector", lambda e: e.tensor_tensor(
                out=endslo[:], in0=ends32[:], in1=wio[0:32, 512:512 + NCH],
                op=OP.subtract), reads=("ends", "wio"), writes=("elo",))
            p.op("vector", lambda e: e.tensor_scalar(
                out=mn8[:].rearrange("p (j l) -> p j l", l=CS),
                in0=tri_sb[:].unsqueeze(1).to_broadcast([128, 8, CS]),
                scalar1=1.0, scalar2=30000.0, op0=OP.subtract, op1=OP.mult),
                reads=("tri", "gsc"), writes=("mn8",))
            if tap == "dt":
                do_tap(acum[:], ("acum",))
                return

            winb_r = winb_d.rearrange("(g p) n -> p g n", p=128)
            for zg in range(4):
                p.dma("scalar", wzb[:].rearrange("p (g n) -> p g n", n=512),
                      winb_r[:, :, zg * 512:(zg + 1) * 512],
                      reads=("winb",), writes=("wzb",))
                for tc in range(16):
                    q = tc % 4
                    ops = []
                    for dc in range(8):
                        lh = hb[:, dc * T + tc * 128: dc * T + tc * 128 + 128]
                        ops.append(mmf(lh, wzb[:, dc * 512:(dc + 1) * 512],
                                       P1[:, q * 512:(q + 1) * 512],
                                       dc == 0, dc == 7))
                    p.op("tensor", ops, reads=("wzb", "h"), writes=(P1r[q],))
                    p.op("scalar", lambda e, q=q: e.activation(
                        out=gbuf[:, q * 512:(q + 1) * 512],
                        in_=P1[:, q * 512:(q + 1) * 512],
                        func=AF.Silu), reads=(P1r[q],), writes=("gsc",))
                    p.dma("gpsimd",
                          zsil_d[tc * 128:(tc + 1) * 128, zg * 512:(zg + 1) * 512],
                          gbuf[:, q * 512:(q + 1) * 512],
                          reads=("gsc",), writes=("zsil",))

            for och in range(18):
                slot = och % 2
                p.dma("scalar",
                      wxb[:, slot * 1024: slot * 1024 + 1024]
                      .rearrange("p (g n) -> p g n", n=128),
                      winb_r[:, :, 2048 + och * 128: 2048 + (och + 1) * 128],
                      reads=("winb",), writes=("wxb%d" % slot,))
                ops = []
                for dc in range(8):
                    lh = wxb[:, slot * 1024 + dc * 128: slot * 1024 + (dc + 1) * 128]
                    for s4 in range(4):
                        ops.append(mmf(lh, hb[:, dc * T + s4 * 512: dc * T + (s4 + 1) * 512],
                                       P1[:, s4 * 512:(s4 + 1) * 512], dc == 0, dc == 7))
                p.op("tensor", ops, reads=("wxb%d" % slot, "h"), writes=P1r)
                p.op("vector", lambda e: e.tensor_copy(out=yraw[:, 0:T], in_=P1[:]),
                     reads=P1r, writes=("ysc",))
                p.dma("gpsimd", xbcpre_d[och * 128:(och + 1) * 128, :], yraw[:, 0:T],
                      reads=("ysc",), writes=("xbcpre",))
            if tap == "inproj":
                do_tap(hb[:], ("h",))
                return

            for och in range(18):
                p.dma("sync", prebuf[:, 3:3 + T],
                      xbcpre_d[och * 128:(och + 1) * 128, :],
                      reads=("xbcpre",), writes=("prebuf",))
                p.op("vector", lambda e: e.memset(ap=prebuf[:, 0:3], constant=0.0),
                     writes=("prebuf",))
                slot = och % 2
                p.dma("scalar", wio[:, 0:512].rearrange("p (i q) -> p i q", q=128),
                      cdiag.rearrange("p (i c q) -> p c i q", c=18, q=128)[:, och, :, :],
                      reads=(), writes=("wio",))
                p.op("vector", lambda e, s=slot: e.tensor_copy(
                    out=cdg[:, s * 512:(s + 1) * 512], in_=wio[:, 0:512]),
                    reads=("wio",), writes=("cdg%d" % slot,))
                ops = []
                for s4 in range(4):
                    for i in range(4):
                        ops.append(mmf(cdg[:, slot * 512 + i * 128: slot * 512 + (i + 1) * 128],
                                       prebuf[:, s4 * 512 + i: s4 * 512 + i + 512],
                                       P1[:, s4 * 512:(s4 + 1) * 512], i == 0, i == 3))
                p.op("tensor", ops, reads=("cdg%d" % slot, "prebuf"), writes=P1r)
                dsttile = bfm if och == 16 else (cfm if och == 17 else prebuf[:, 3:3 + T])
                dres = "bfm" if och == 16 else ("cfm" if och == 17 else "prebuf")
                p.op("scalar", lambda e, d=dsttile, c=cb_sb[:, och:och + 1]:
                     e.activation(out=d[:], in_=P1[:], func=AF.Silu, bias=c, scale=1.0),
                     reads=P1r + ("cb",), writes=(dres,))
                if och < 17:
                    for grp in range(2):
                        ops = []
                        for k in range(8):
                            tc = grp * 8 + k
                            ops.append(lambda e, s=dsttile[:, tc * 128:(tc + 1) * 128],
                                       o=PT[:, k * 128:(k + 1) * 128]:
                                       e.transpose(out=o, in_=s, identity=idb_sb[:]))
                        p.op("tensor", ops, reads=(dres, "idb"), writes=("PT",))
                        if och < 16:
                            p.op("vector", lambda e: e.tensor_copy(
                                out=yraw[:, 0:1024], in_=PT[:]),
                                reads=("PT",), writes=("ysc",))
                            for k in range(8):
                                tc = grp * 8 + k
                                p.dma("gpsimd",
                                      xtm_d[tc * 128:(tc + 1) * 128,
                                            och * 128:(och + 1) * 128],
                                      yraw[:, k * 128:(k + 1) * 128],
                                      reads=("ysc",), writes=("xtm%d" % tc,))
                        else:
                            p.op("vector",
                                 lambda e, d=btm[:, grp * 1024:(grp + 1) * 1024]:
                                 e.tensor_copy(out=d, in_=PT[:]),
                                 reads=("PT",), writes=("btm",))
            if tap == "conv":
                do_tap(cfm[:], ("cfm",))
                return


            p.op("vector", lambda e: e.tensor_copy(out=nbb_sb[:], in_=nb_sb[:]),
                 reads=("nb",), writes=("nbb",))
            for dcg in range(16):
                slot = dcg % 2
                wres = "wzb" if slot == 0 else "wzc"
                rex = ("zsil",) if dcg == 1 else ()
                p.dma("scalar", wzb[:, slot * D: slot * D + D],
                      woutb_d[dcg * 128:(dcg + 1) * 128, :],
                      reads=("woutbd",) + rex, writes=(wres,))
                ops = [mmf(wzb[:, slot * D + och * 128: slot * D + och * 128 + 128],
                           nbb_sb[:, dcg:dcg + 1], P3[:, och:och + 1],
                           dcg == 0 and och == 0, dcg == 15 and och == 7)
                       for och in range(8)]
                p.op("tensor", ops, reads=(wres, "nbb"), writes=("P3",))
            p.op("vector", lambda e: e.tensor_copy(out=ccol[:], in_=P3[:, 0:8]),
                 reads=("P3",), writes=("ccol",))

            first = ["cfm"]
            for src, sres, dhi, dlo, rhi, rlo in [
                    (acumT, "aT", aThi, aTlo, "aThi", "aTlo"),
                    (comboT, "cT", cThi, cTlo, "cThi", "cTlo")]:
                p.op("vector", lambda e, s=src, d=dhi: e.tensor_copy(
                    out=d, in_=s[:]), reads=(sres,) + tuple(first),
                    writes=(rhi,))
                first = []
                p.op("vector", lambda e, d=dhi: e.tensor_copy(
                    out=wio[:, 0:NCH * 32], in_=d),
                    reads=(rhi,), writes=("wio",))
                p.op("vector", lambda e, s=src, d=dlo: e.tensor_tensor(
                    out=d, in0=s[:], in1=wio[:, 0:NCH * 32],
                    op=OP.subtract), reads=(sres, "wio"), writes=(rlo,))

            if tap == "tg":
                do_tap(acumT[:], ("wio",))
            with ExitStack() as sst:
                xtmc = sb(sst, "xtmc", [128, DI], BF16)
                zslc = sb(sst, "zslc", [128, DI], BF16)
                p.op("vector", lambda e: e.memset(ap=rst[:], constant=0.0),
                     writes=("rst",))
                p.op("vector", lambda e: e.memset(ap=rstb[:], constant=0.0),
                     writes=("rstb",))
                for c in range(NCH):
                    cl = slice(c * CS, (c + 1) * CS)
                    sl = 0
                    xtms = xtmc[:]
                    p.dma("sync", xtms, xtm_d[cl, :],
                          reads=("xtm%d" % c,), writes=("xtmc%d" % sl,))
                    p.dma("sync", zslc[:], zsil_d[cl, :],
                          reads=("zsil",), writes=("zslc%d" % sl,))
                    for s2 in range(2):
                        ops = [mmf(cfm[:, cl],
                                   rstb[:, s2 * 1024 + s4 * 512: s2 * 1024 + (s4 + 1) * 512],
                                   P2[:, s4 * 512:(s4 + 1) * 512], True, True)
                               for s4 in range(2)]
                        p.op("tensor", ops, reads=("cfm", "rstb"), writes=("P2",))
                        p.op("vector", lambda e, c=c, s2=s2: e.tensor_tensor(
                            out=xs_sb[:, s2 * 1024:(s2 + 1) * 1024]
                            .rearrange("p (h q) -> p h q", q=64),
                            in0=P2[:].rearrange("p (h q) -> p h q", q=64),
                            in1=expAT[:, c * 32 + s2 * 16: c * 32 + (s2 + 1) * 16]
                            .unsqueeze(2).to_broadcast([128, 16, 64]), op=OP.mult),
                            reads=("P2", "expAT"), writes=("xs",))
                    if tap == "tyo%d" % c:
                        do_tap(xs_sb[:], ("xs",))
                    p.op("tensor", mmf(bfm[:, cl], cfm[:, cl], P3[:, 0:128], True, True),
                         reads=("bfm", "cfm"), writes=("P3",))
                    p.op("vector", lambda e: e.tensor_tensor(
                        out=gm_sb[:], in0=P3[:, 0:128], in1=tri_sb[:], op=OP.mult),
                        reads=("P3", "tri"), writes=("gm",))
                    if tap == "tGm%d" % c:
                        do_tap(gm_sb[:], ("gm",))
                    for hf in range(4):
                        par = hf % 2
                        slot = wiob[:, par * 2048: par * 2048 + 2048]
                        sres = "wio" if par == 0 else "wioB"
                        esbX = esb if par == 0 else esb2
                        eres = "gse" if par == 0 else "gse2"
                        a8 = slice(c * 32 + hf * 8, c * 32 + hf * 8 + 8)
                        p.op("vector", lambda e, s=slot, a=a8: e.tensor_tensor(
                            out=s[:, 0:1024].rearrange("p (j l) -> p j l", l=CS),
                            in0=idb_sb[:].unsqueeze(1).to_broadcast([128, 8, CS]),
                            in1=aThi[:, a].unsqueeze(2).to_broadcast([128, 8, CS]),
                            op=OP.mult), reads=("idb", "aThi"), writes=(sres,))
                        p.op("vector", lambda e, s=slot, a=a8: e.tensor_tensor(
                            out=s[:, 1024:2048].rearrange("p (j l) -> p j l", l=CS),
                            in0=idb_sb[:].unsqueeze(1).to_broadcast([128, 8, CS]),
                            in1=aTlo[:, a].unsqueeze(2).to_broadcast([128, 8, CS]),
                            op=OP.mult), reads=("idb", "aTlo"), writes=(sres,))
                        for half in range(2):
                            bank = P3[:, 0:512] if half == 0 else PTf32[:, 0:512]
                            bres = "P3" if half == 0 else "PT"
                            c4 = slice(c * 32 + hf * 8 + half * 4,
                                       c * 32 + hf * 8 + half * 4 + 4)
                            h0 = half * 512
                            ops = [
                                mmf(onesb_sb[:], slot[:, h0:h0 + 512],
                                    bank, True, False),
                                mmf(onesb_sb[:], slot[:, 1024 + h0:1024 + h0 + 512],
                                    bank, False, False),
                                mmf(idb_sb[:], cThi[:, c4].unsqueeze(2)
                                    .to_broadcast([128, 4, CS]), bank, False, False),
                                mmf(idb_sb[:], cTlo[:, c4].unsqueeze(2)
                                    .to_broadcast([128, 4, CS]), bank, False, False),
                                mmf(idb_sb[:], mn8[:, h0:h0 + 512],
                                    bank, False, True)]
                            p.op("tensor", ops,
                                 reads=(sres, "cThi", "cTlo", "mn8", "onesb", "idb"),
                                 writes=(bres,))
                            p.op("scalar", lambda e, b=bank,
                                 o=esbX[:, h0:h0 + 512]:
                                 e.activation(out=o, in_=b, func=AF.Exp),
                                 reads=(bres,), writes=(eres,))
                        if tap == "tE%d_%d" % (c, hf):
                            do_tap(esbX[:], (eres,))
                        p.op("vector", lambda e, s=esbX: e.tensor_tensor(
                            out=s[:].rearrange("p (h l) -> p h l", l=128),
                            in0=s[:].rearrange("p (h l) -> p h l", l=128),
                            in1=gm_sb[:].unsqueeze(1).to_broadcast([128, 8, 128]),
                            op=OP.mult), reads=(eres, "gm"), writes=(eres,))
                        ops = []
                        for k in range(8):
                            h = hf * 8 + k
                            ops.append(mmf(esbX[:, k * 128:(k + 1) * 128],
                                           xtms[:, h * 64:(h + 1) * 64],
                                           P1[:, h * 64:(h + 1) * 64],
                                           k == 0, k == 7))
                        p.op("tensor", ops, reads=(eres, "xtmc%d" % sl),
                             writes=(P1r[hf],))
                    p.op("vector", lambda e: e.tensor_tensor(
                        out=ygt[:], in0=P1[:], in1=xs_sb[:], op=OP.add),
                        reads=P1r + ("xs",), writes=("ygt",))
                    p.op("vector", lambda e, z=zslc[:]: e.tensor_tensor(
                        out=ygt[:], in0=ygt[:], in1=z, op=OP.mult),
                        reads=("ygt", "zslc%d" % sl), writes=("ygt",))
                    if tap == "yg%d" % c:
                        do_tap(ygt[:], ("ygt",))
                    if tap == "yo%d" % c:
                        do_tap(xs_sb[:], ("xs",))
                    p.op("vector", lambda e: e.tensor_scalar(
                        out=yraw[:, 0:DI], in0=ygt[:], scalar1=1.0, scalar2=None,
                        op0=OP.mult, op1=OP.add, accum_out=sumc[:]),
                        reads=("ygt",), writes=("sumc", "ysc"))
                    p.op("scalar", lambda e: e.activation(
                        out=yraw[:, 0:DI], in_=ygt[:], func=AF.Square),
                        reads=("ygt",), writes=("ysc",))
                    p.op("vector", lambda e: e.tensor_scalar(
                        out=yraw[:, 0:DI], in0=yraw[:, 0:DI], scalar1=1.0,
                        scalar2=None, op0=OP.mult, op1=OP.add, accum_out=sqc[:]),
                        reads=("ysc",), writes=("sqc", "ysc"))
                    p.op("vector", lambda e: e.tensor_scalar(
                        out=mucol[:], in0=sumc[:], scalar1=-1.0 / DI, scalar2=None,
                        op0=OP.mult), reads=("sumc",), writes=("mucol",))
                    p.op("scalar", lambda e: e.activation(
                        out=m2col[:], in_=sumc[:], func=AF.Square, scale=1.0 / DI),
                        reads=("sumc",), writes=("m2col",))
                    p.op("vector", lambda e: e.tensor_scalar(
                        out=varcol[:], in0=sqc[:], scalar1=1.0 / DI, scalar2=float(EPS),
                        op0=OP.mult, op1=OP.add), reads=("sqc",), writes=("varcol",))
                    p.op("vector", lambda e: e.tensor_tensor(
                        out=varcol[:], in0=varcol[:], in1=m2col[:], op=OP.subtract),
                        reads=("varcol", "m2col"), writes=("varcol",))
                    p.op("scalar", lambda e: e.activation(
                        out=varcol[:], in_=varcol[:], func=AF.Ln),
                        reads=("varcol",), writes=("varcol",))
                    p.op("scalar", lambda e: e.activation(
                        out=rcol[:], in_=varcol[:], func=AF.Exp, scale=-0.5),
                        reads=("varcol",), writes=("rcol",))
                    p.op("vector", lambda e: e.tensor_tensor(
                        out=nmrcol[:], in0=mucol[:], in1=rcol[:], op=OP.mult),
                        reads=("mucol", "rcol"), writes=("nmrcol",))
                    p.op("vector", lambda e: e.tensor_scalar(
                        out=ygt[:], in0=ygt[:], scalar1=rcol[:], scalar2=nmrcol[:],
                        op0=OP.mult, op1=OP.add),
                        reads=("ygt", "rcol", "nmrcol"), writes=("ygt",))
                    if tap == "yln%d" % c:
                        do_tap(ygt[:], ("ygt",))
                        return
                    for grp in range(2):
                        ops = []
                        for k in range(8):
                            dcx = grp * 8 + k
                            ops.append(lambda e, s=ygt[:, dcx * 128:(dcx + 1) * 128],
                                       o=PT[:, k * 128:(k + 1) * 128]:
                                       e.transpose(out=o, in_=s, identity=idb_sb[:]))
                        p.op("tensor", ops, reads=("ygt", "idb"), writes=("PT",))
                        p.op("vector", lambda e, c=c, grp=grp: e.tensor_copy(
                            out=gbuf[:, 0:4096].rearrange("p (d q) -> p d q", q=256)
                            [:, grp * 8:(grp + 1) * 8, (c % 2) * 128:(c % 2) * 128 + 128],
                            in_=PT[:].rearrange("p (d q) -> p d q", q=128)),
                            reads=("PT",), writes=("gsc",))
                    p.op("vector", lambda e, c=c: e.tensor_tensor(
                        out=xs_sb[:].rearrange("p (h q) -> p h q", q=64),
                        in0=xtms.rearrange("p (h q) -> p h q", q=64),
                        in1=vT[:, c * 32:(c + 1) * 32].unsqueeze(2)
                        .to_broadcast([128, 32, 64]), op=OP.mult),
                        reads=("xtmc%d" % sl, "vTT"), writes=("xs",))
                    p.op("vector", lambda e, c=c: e.tensor_tensor(
                        out=gdthi, in0=idb_sb[0:32, 0:32],
                        in1=endshi[:, c:c + 1].to_broadcast([32, 32]),
                        op=OP.mult), reads=("idb", "ehi"), writes=("gdt",))
                    p.op("vector", lambda e, c=c: e.tensor_tensor(
                        out=gdtlo, in0=idb_sb[0:32, 0:32],
                        in1=endslo[:, c:c + 1].to_broadcast([32, 32]),
                        op=OP.mult), reads=("idb", "elo"), writes=("gdt",))
                    p.op("tensor",
                         [mmf(onesb_sb[0:32, :], gdthi, P3[:, 128:160],
                              True, False),
                          mmf(onesb_sb[0:32, :], gdtlo, P3[:, 128:160],
                              False, True)],
                         reads=("gdt", "onesb"), writes=("P3",))
                    p.op("scalar", lambda e: e.activation(
                        out=gamrep[:], in_=P3[:, 128:160], func=AF.Exp),
                        reads=("P3",), writes=("gamrep",))
                    if tap == "tgam%d" % c:
                        do_tap(gamrep[:], ("gamrep",))
                    p.op("gpsimd", lambda e: e.tensor_tensor(
                        out=rst[:].rearrange("p (h q) -> p h q", q=64),
                        in0=rst[:].rearrange("p (h q) -> p h q", q=64),
                        in1=gamrep[:].unsqueeze(2).to_broadcast([128, 32, 64]),
                        op=OP.mult), reads=("rst", "gamrep"), writes=("rst",))
                    for s2 in range(2):
                        ops = [mmf(btm[:, c * 128:(c + 1) * 128],
                                   xs_sb[:, s2 * 1024 + s4 * 512:
                                         s2 * 1024 + (s4 + 1) * 512],
                                   P2[:, s4 * 512:(s4 + 1) * 512], True, True)
                               for s4 in range(2)]
                        p.op("tensor", ops, reads=("btm", "xs"), writes=("P2",))
                        p.op("vector", lambda e, s2=s2: e.tensor_tensor(
                            out=rst[:, s2 * 1024:(s2 + 1) * 1024],
                            in0=rst[:, s2 * 1024:(s2 + 1) * 1024],
                            in1=P2[:], op=OP.add),
                            reads=("P2", "rst"), writes=("rst",))
                    p.op("vector", lambda e: e.tensor_copy(out=rstb[:], in_=rst[:]),
                         reads=("rst",), writes=("rstb",))
                    if c % 2 == 1:
                        gidx = c // 2
                        wsrc = woutb_d.rearrange("(g p) n -> p g n", p=128)
                        for och in range(8):
                            q = och % 4
                            osl = slice(och * 128, (och + 1) * 128)
                            if och % 2 == 0:
                                p.dma("scalar",
                                      wob[:].rearrange("p (g n) -> p g n", n=128),
                                      wsrc[:, :, osl],
                                      reads=("woutbd",), writes=("wo0",))
                                wt = [wob[:, dcg * 128:(dcg + 1) * 128]
                                      for dcg in range(16)]
                                rds = ("wo0",)
                            else:
                                rex = ("aTlo", "cTlo", "aThi", "cThi") \
                                    if c == 1 and och == 1 else ()
                                p.dma("scalar",
                                      wob2a.rearrange("p (g n) -> p g n", n=128),
                                      wsrc[:, 0:8, osl],
                                      reads=("woutbd",) + rex, writes=("wo1",))
                                p.dma("scalar",
                                      wob2b.rearrange("p (g n) -> p g n", n=128),
                                      wsrc[:, 8:16, osl],
                                      reads=("woutbd",), writes=("wo1",))
                                wt = [(wob2a if dcg < 8 else wob2b)
                                      [:, (dcg % 8) * 128:(dcg % 8) * 128 + 128]
                                      for dcg in range(16)]
                                rds = ("wo1",)
                            ops = [mmf(wt[dcg],
                                       gbuf[:, dcg * 256:(dcg + 1) * 256],
                                       P1[:, q * 512:q * 512 + 256],
                                       dcg == 0, dcg == 15)
                                   for dcg in range(16)]
                            p.op("tensor", ops, reads=rds + ("gsc",),
                                 writes=(P1r[q],))
                            ssl = xstage[:, (och % 2) * NT: (och % 2) * NT + 256]
                            xres = "xs%d" % (och % 2)
                            dsl = out_t[och * 128:(och + 1) * 128,
                                        gidx * 256:(gidx + 1) * 256]
                            p.dma("sync", ssl, dsl, reads=("xd",), writes=(xres,))
                            p.op("vector", lambda e, x=ssl, q=q, o=och:
                                 e.scalar_tensor_tensor(
                                     out=x, in0=P1[:, q * 512:q * 512 + 256],
                                     scalar=ccol[:, o:o + 1], in1=x,
                                     op0=OP.add, op1=OP.add),
                                 reads=(P1r[q], "ccol", xres), writes=(xres,))
                            p.dma("gpsimd", dsl, ssl, reads=(xres,), writes=("xd",))
            for k in range(16):
                dc, H = k // 2, k % 2
                ssl = xstage[:, (k % 2) * NT:(k % 2 + 1) * NT]
                xres = "xs%d" % (k % 2)
                p.dma("sync", ssl, out_t[dc * 128:(dc + 1) * 128, H * NT:(H + 1) * NT],
                      reads=("xd",), writes=(xres,))
                p.op("vector", lambda e, x=ssl,
                     d=xb[:, dc * T + H * NT: dc * T + H * NT + NT]:
                     e.tensor_copy(out=d, in_=x), reads=(xres,), writes=("xb",))
            if tap == "xmix":
                do_tap(xb[:], ("xb",))
                return

    emit_ffn(0)
    if not p.dead:
        p.barrier()
    emit_mixer()
    if not p.dead:
        p.barrier()
    emit_ffn(1)
    p.final_wait("gpsimd", ("xd",))
    p.op("gpsimd", lambda e: e.nop())
    p.realize()
    st.close()
    return nc


def host_inputs(inputs):
    u = np.asarray(inputs["L0_attn_u"])
    w_in = np.ascontiguousarray(inputs["L0_attn_W_in"], np.float32)
    conv_w = np.asarray(inputs["L0_attn_conv_w"])
    conv_b = np.asarray(inputs["L0_attn_conv_b"])
    dt_bias = np.asarray(inputs["L0_attn_dt_bias"]).reshape(32, 1).astype(np.float32)
    a_log = np.asarray(inputs["L0_attn_A_log"]).reshape(32, 1).astype(np.float32)
    norm_w = np.asarray(inputs["L0_attn_norm_w"])
    norm_b = np.asarray(inputs["L0_attn_norm_b"])
    w_out = np.ascontiguousarray(inputs["L0_attn_W_out"], np.float32)
    qw = np.asarray(inputs["L0_ffn_qweights"])
    sc = np.asarray(inputs["L0_ffn_scales"])
    bi = np.asarray(inputs["L0_ffn_biases"], np.float32)
    la = np.asarray(inputs["L0_ffn_lora_a"])
    lb = np.asarray(inputs["L0_ffn_lora_b"])
    lnw = np.asarray(inputs["L0_ffn_ln_gamma"])
    lnb = np.asarray(inputs["L0_ffn_ln_beta"])

    wqt = np.ascontiguousarray(qw.transpose(0, 2, 1), np.int32)
    srt = np.ascontiguousarray(np.repeat(sc, 16, axis=2).transpose(0, 2, 1),
                               np.float32)
    biasc = np.ascontiguousarray(
        bi.reshape(6, 8, 128).transpose(2, 0, 1).reshape(128, 48), np.float32)
    lat = np.ascontiguousarray(
        la.transpose(0, 2, 1).reshape(6, 8, 128, 32).transpose(2, 0, 1, 3)
        .reshape(128, 6 * 8 * 32), np.float32)
    lbt = np.ascontiguousarray(
        lb.transpose(0, 2, 1).transpose(1, 0, 2).reshape(32, 6 * D), np.float32)
    lnw3 = np.ascontiguousarray(
        lnw.reshape(3, 8, 128).transpose(2, 0, 1).reshape(128, 24), np.float32)
    lnb3 = np.ascontiguousarray(
        lnb.reshape(3, 8, 128).transpose(2, 0, 1).reshape(128, 24), np.float32)
    nwc = np.ascontiguousarray(norm_w.reshape(16, 128).T, np.float32)
    nbc = np.ascontiguousarray(norm_b.reshape(16, 128).T, np.float32)
    cb = np.ascontiguousarray(conv_b.reshape(18, 128).T, np.float32)
    cd = np.zeros((128, 4 * 18 * 128), np.float32)
    for i in range(4):
        for c in range(18):
            blk = cd[:, (i * 18 + c) * 128:(i * 18 + c + 1) * 128]
            np.fill_diagonal(blk, conv_w[c * 128:(c + 1) * 128, i])
    identb = np.eye(128, dtype=ml_dtypes.bfloat16)
    identf = np.eye(128, dtype=np.float32)
    tri = np.triu(np.ones((128, 128), np.float32)).copy()
    onesbf = np.ones((128, 128), dtype=ml_dtypes.bfloat16)
    onesf = np.ones((128, 512), np.float32)

    shared = dict(win=w_in, wqt=wqt, srt=srt, biasc=biasc, lat=lat, lbt=lbt,
                  lnw3=lnw3, lnb3=lnb3, wout=w_out, nwc=nwc, nbc=nbc,
                  cdiag=cd, cbias=cb, dtb=dt_bias, alog=a_log,
                  identb=identb, identf=identf, trimask=tri,
                  onesbf=onesbf, onesf=onesf)
    in_maps = []
    for b in range(8):
        m = dict(shared)
        m["xin"] = np.ascontiguousarray(u[b].T, np.float32)
        in_maps.append(m)
    return in_maps


def kernel(**inputs):
    in_maps = host_inputs(inputs)
    nc = build_program()
    res = run_bass_kernel_spmd(nc, in_maps, core_ids=list(range(8)))
    out = np.stack([np.asarray(r["out"]).T for r in res.results])
    return out.astype(np.float32)
